# version 13
# baseline (speedup 1.0000x reference)
import numpy as np
from contextlib import ExitStack

import concourse.bass as bass
import concourse.mybir as mybir
from concourse.bass_utils import run_bass_kernel_spmd

F32 = mybir.dt.float32
BF16 = mybir.dt.bfloat16
AF = mybir.ActivationFunctionType
ALU = mybir.AluOpType
AX = mybir.AxisListType

D = 1024
KC = 8
NST = 16
EXP = 4
H = 5632
HC = 44
BC = 2
T = 2048
TOK = BC * T
NTT = 8
TTS = 512
CT = 32
NCH = T // CT

SC_C = (0.1 + 0.001) / 2.0
SC_R = (0.1 - 0.001) / 2.0
SC_SCALE = 1.0 / (SC_R + 1e-8)
SC_BIAS = -SC_C * SC_SCALE
LN20 = float(np.log(20.0))


class Sem:
    def __init__(self, es, nc, name):
        self.h = es.enter_context(nc.semaphore(name))
        self.n = 0

    def bump(self, k=1):
        self.n += k
        return self.n


class Plan:
    ENGINES = ("sync", "gpsimd", "vector", "scalar", "tensor")

    def __init__(self, block):
        self.block = block
        self.streams = {e: [] for e in self.ENGINES}

    def op(self, eng, fn, waits=(), inc=None, inc_k=1):
        wl = [(s.h, v) for (s, v) in waits]
        tgt = None
        if inc is not None:
            inc.bump(inc_k)
            tgt = (inc.h, inc_k)

        def run(e, fn=fn, wl=wl, tgt=tgt):
            for h, v in wl:
                e.wait_ge(h, v)
            inst = fn(e)
            if tgt is not None:
                if inst is None:
                    inst = e.nop()
                inst.then_inc(*tgt)
        self.streams[eng].append(run)

    def selfsync(self, eng, sem):
        k = sem.bump()

        def run(e, h=sem.h, k=k):
            e.nop().then_inc(h, 1)
            e.wait_ge(h, k)
        self.streams[eng].append(run)

    def wait(self, eng, sem, val=None):
        v = sem.n if val is None else val
        self.streams[eng].append(lambda e, h=sem.h, v=v: e.wait_ge(h, v))

    def flush(self):
        for name in self.ENGINES:
            fns = self.streams[name]
            if not fns:
                continue
            section = getattr(self.block, name)

            def body(eng, fns=fns):
                for f in fns:
                    f(eng)
            section(body)
            self.streams[name] = []


def ap_of(tile, offset_cols, dims):
    a = tile.ap()
    part = list(a.ap[0])
    return bass.AP(a.tensor, a.offset + offset_cols,
                   [part] + [list(d) for d in dims])


def build_program(parts="all"):
    nc = bass.Bass("TRN2", target_bir_lowering=False)

    x_h = nc.dram_tensor("x", [D, BC, T], F32, kind="ExternalInput").ap()
    w13_h = nc.dram_tensor("w13", [2, EXP, HC, 128, KC, 128], F32, kind="ExternalInput").ap()
    w2_h = nc.dram_tensor("w2", [EXP, KC, 128, HC * 128], F32, kind="ExternalInput").ap()
    wdt_h = nc.dram_tensor("wdt", [D, D], F32, kind="ExternalInput").ap()
    wb_h = nc.dram_tensor("wb", [D, NST], F32, kind="ExternalInput").ap()
    wc_h = nc.dram_tensor("wc", [D, NST], F32, kind="ExternalInput").ap()
    gw_h = nc.dram_tensor("gw", [D, EXP], F32, kind="ExternalInput").ap()
    alog_h = nc.dram_tensor("alog", [D, NST], F32, kind="ExternalInput").ap()
    bdt_h = nc.dram_tensor("bdt", [128, KC], F32, kind="ExternalInput").ap()
    dp_h = nc.dram_tensor("dp", [128, KC], F32, kind="ExternalInput").ap()
    out_h = nc.dram_tensor("out", [D, BC, T], F32, kind="ExternalOutput").ap()
    dbg_h = (nc.dram_tensor("dbg", [128, 32768], F32, kind="ExternalOutput").ap()
             if parts == "f1dbg" else None)

    x1s = nc.dram_tensor("x1s", [D, BC, T], F32, kind="Internal").ap()
    x2s = nc.dram_tensor("x2s", [D, BC, T], F32, kind="Internal").ap()
    hds = nc.dram_tensor("hds", [EXP, NTT, 128, HC, TTS], BF16, kind="Internal").ap()
    w2b = nc.dram_tensor("w2b", [EXP, KC, 128, HC * 128], BF16, kind="Internal").ap()
    cwd = nc.dram_tensor("cwd", [EXP, TOK], F32, kind="Internal").ap()
    bcd = nc.dram_tensor("bcd", [2, NCH, BC * CT * NST], F32, kind="Internal").ap()

    with ExitStack() as es:
        s_ldx = Sem(es, nc, "ldx")
        s_ldw = Sem(es, nc, "ldw")
        s_st = Sem(es, nc, "st")
        s_t = Sem(es, nc, "t")
        s_v = Sem(es, nc, "v")
        s_a = Sem(es, nc, "a")
        s_g = Sem(es, nc, "g")
        s_ph = Sem(es, nc, "ph")
        s_xt = Sem(es, nc, "xt")
        s_w13 = [Sem(es, nc, f"w13_{i}") for i in range(2)]
        s_cwb = Sem(es, nc, "cwb")
        s_pc = Sem(es, nc, "pc")
        s_hrd = [Sem(es, nc, f"hrd{i}") for i in range(2)]
        s_w2s = [Sem(es, nc, f"w2s{i}") for i in range(2)]
        s_hst = [Sem(es, nc, f"hst{i}") for i in range(2)]
        s_x1st = Sem(es, nc, "x1st")
        s_cwdst = Sem(es, nc, "cwdst")
        s_x1c = Sem(es, nc, "x1c")
        s_bst = Sem(es, nc, "bst")
        s_bc = [Sem(es, nc, f"bc{i}") for i in range(2)]
        s_x2st = Sem(es, nc, "x2st")
        store_sems = [s_st, s_hst[0], s_hst[1], s_x1st, s_cwdst, s_bst, s_x2st]

        def sb(name, shape, dt=F32):
            return es.enter_context(nc.sbuf_tensor(name, shape, dt))

        ones_f = sb("ones_f", [128, 128])
        identf = sb("identf", [128, 128])
        gw_sb = sb("gw_sb", [128, KC * EXP], BF16)
        wb_sb = sb("wb_sb", [128, KC * NST], BF16)
        wc_sb = sb("wc_sb", [128, KC * NST], BF16)
        pa_sb = sb("pa_sb", [128, KC * NST])
        bdt_sb = sb("bdt_sb", [128, KC])
        dp_sb = sb("dp_sb", [128, KC])
        gmask = sb("gmask", [128, EXP * EXP])
        cst = sb("cst", [128, 8])

        block = es.enter_context(nc.Block())
        P = Plan(block)
        nbar = [0]

        def barrier():
            for ss_ in store_sems:
                P.wait("scalar", ss_, ss_.n)
            nbar[0] += 1
            for eng in Plan.ENGINES:
                P.op(eng, lambda e: e.nop(), inc=s_ph)
            tgt = s_ph.n
            for eng in Plan.ENGINES:
                P.wait(eng, s_ph, tgt)

        if True:
            P.op("gpsimd", lambda e: e.memset(ones_f[:], 1.0))
            P.op("gpsimd", lambda e: e.memset(identf[:], 1.0))
            P.op("gpsimd", lambda e: e.affine_select(
                identf[:], identf[:], [[1, 128]], ALU.is_equal, 0.0,
                base=0, channel_multiplier=-1))
            for j in range(EXP):
                for ee in range(EXP):
                    P.op("gpsimd", lambda e, j=j, ee=ee: e.memset(
                        gmask[:, j * EXP + ee:j * EXP + ee + 1], 1.0 if ee > j else 0.0))
            for i, cv in enumerate([LN20, -0.5, 1e-5, 1.0, 16.0, -32.0, SC_BIAS, -LN20]):
                P.op("gpsimd", lambda e, i=i, cv=cv: e.memset(cst[:, i:i + 1], cv))
        if True:
            P.op("gpsimd", lambda e: e.dma_start(
                out=gw_sb[:].rearrange("p (k e) -> p k e", e=EXP),
                in_=gw_h.rearrange("(k p) e -> p k e", p=128)), inc=s_ldw, inc_k=16)
            P.op("gpsimd", lambda e: e.dma_start(
                out=wb_sb[:].rearrange("p (k n) -> p k n", n=NST),
                in_=wb_h.rearrange("(k p) n -> p k n", p=128)), inc=s_ldw, inc_k=16)
            P.op("gpsimd", lambda e: e.dma_start(
                out=wc_sb[:].rearrange("p (k n) -> p k n", n=NST),
                in_=wc_h.rearrange("(k p) n -> p k n", p=128)), inc=s_ldw, inc_k=16)
            P.op("sync", lambda e: e.dma_start(
                out=pa_sb[:].rearrange("p (k n) -> p k n", n=NST),
                in_=alog_h.rearrange("(k p) n -> p k n", p=128)), inc=s_ldx, inc_k=16)
            P.op("sync", lambda e: e.dma_start(out=bdt_sb[:], in_=bdt_h), inc=s_ldx, inc_k=16)
            P.op("sync", lambda e: e.dma_start(out=dp_sb[:], in_=dp_h), inc=s_ldx, inc_k=16)
            P.wait("scalar", s_ldx)
            P.op("scalar", lambda e: e.activation(pa_sb[:], pa_sb[:], AF.Tanh, scale=0.1))
            P.selfsync("scalar", s_a)
            P.op("scalar", lambda e: e.activation(pa_sb[:], pa_sb[:], AF.Exp, scale=10.0),
                 inc=s_a)
            P.wait("tensor", s_ldw)
            P.wait("tensor", s_ldx)
        barrier()
        with nc.named_scope("p0"):
            P.flush()

        w2cast_jobs = [(ee, dc) for ee in range(EXP) for dc in range(KC)]

        def emit_ffn(src_x, dst_x, tag, precast):
            with ExitStack() as fes:
                def fsb(name, shape, dt=F32):
                    return fes.enter_context(nc.sbuf_tensor(tag + name, shape, dt))

                u_bf = fsb("u", [128, KC * TOK], BF16)
                xt = fsb("xt", [128, KC * TTS])
                sqb = [fsb(f"sq{i}", [128, TTS]) for i in range(2)]
                mu_t = fsb("mu", [128, TTS])
                rs_t = fsb("rs", [128, TTS])
                jk = fsb("jk", [128, TTS])
                lsb = fsb("lsb", [EXP, TTS])
                ltm = fsb("ltm", [128, EXP])
                exm = fsb("exm", [128, EXP])
                cnt = fsb("cnt", [128, EXP])
                tm4 = fsb("tm4", [128, EXP])
                mxc = fsb("mxc", [128, 2])
                rowb = [fsb(f"row{e}", [1, TTS]) for e in range(EXP)]
                ps_s1 = fes.enter_context(nc.psum_tensor(tag + "ps1", [128, TTS], F32))
                ps_s2 = fes.enter_context(nc.psum_tensor(tag + "ps2", [128, TTS], F32))
                ps_g = fes.enter_context(nc.psum_tensor(tag + "psg", [128, TTS], F32))

                def uslice(kc, tt):
                    return u_bf[:, kc * TOK + tt * TTS: kc * TOK + (tt + 1) * TTS]

                for tt in range(NTT):
                    b, t0 = tt // 4, (tt % 4) * TTS
                    xw = ((s_v, s_v.n),) if tt > 0 else ()
                    for kc in range(KC):
                        P.op("sync", lambda e, kc=kc, b=b, t0=t0: e.dma_start(
                            out=xt[:, kc * TTS:(kc + 1) * TTS],
                            in_=src_x[kc * 128:(kc + 1) * 128, b, t0:t0 + TTS]),
                            waits=(xw if kc == 0 else ()), inc=s_xt, inc_k=16)
                    x_done = s_xt.n
                    P.wait("tensor", s_xt, x_done)
                    if tt > 0:
                        P.wait("tensor", s_v, s_v.n)
                    for kc in range(KC):
                        P.op("tensor", lambda e, kc=kc: e.matmul(
                            ps_s1[:], ones_f[:], xt[:, kc * TTS:(kc + 1) * TTS],
                            start=(kc == 0), stop=(kc == KC - 1)),
                            inc=(s_t if kc == KC - 1 else None))
                    P.wait("scalar", s_xt, x_done)
                    for kc in range(KC):
                        w = ((s_t, s_t.n),) if kc >= 2 else ()
                        P.op("scalar", lambda e, kc=kc: e.activation(
                            sqb[kc % 2][:], xt[:, kc * TTS:(kc + 1) * TTS], AF.Square),
                            waits=w, inc=s_a)
                        P.op("tensor", lambda e, kc=kc: e.matmul(
                            ps_s2[:], ones_f[:], sqb[kc % 2][:],
                            start=(kc == 0), stop=(kc == KC - 1)),
                            waits=((s_a, s_a.n),), inc=s_t)
                    stats_done = s_t.n
                    P.wait("vector", s_t, stats_done)
                    P.op("vector", lambda e: e.tensor_scalar(
                        mu_t[:], ps_s1[:], 1.0 / D, None, ALU.mult))
                    P.op("vector", lambda e: e.tensor_scalar(
                        rs_t[:], ps_s2[:], 1.0 / D, None, ALU.mult), inc=s_v)
                    P.wait("vector", s_v)
                    P.op("vector", lambda e: e.tensor_tensor(jk[:], mu_t[:], mu_t[:], ALU.mult),
                         inc=s_v)
                    P.wait("vector", s_v)
                    P.op("vector", lambda e: e.tensor_tensor(
                        rs_t[:], rs_t[:], jk[:], ALU.subtract), inc=s_v)
                    P.wait("scalar", s_v)
                    P.op("scalar", lambda e: e.activation(rs_t[:], rs_t[:], AF.Ln, bias=cst[:, 2:3]),
                         inc=s_a)
                    P.wait("scalar", s_a)
                    P.op("scalar", lambda e: e.activation(rs_t[:], rs_t[:], AF.Exp, scale=cst[:, 1:2]),
                         inc=s_a)
                    P.wait("vector", s_a)
                    for kc in range(KC):
                        P.op("vector", lambda e, kc=kc: e.tensor_tensor(
                            jk[:], xt[:, kc * TTS:(kc + 1) * TTS], mu_t[:], ALU.subtract),
                            inc=s_v)
                        P.wait("vector", s_v)
                        P.op("vector", lambda e, kc=kc, tt=tt: e.tensor_tensor(
                            uslice(kc, tt), jk[:], rs_t[:], ALU.mult),
                            inc=(s_v if kc == KC - 1 else None))
                    u_done = s_v.n
                    P.wait("tensor", s_v, u_done)
                    for kc in range(KC):
                        P.op("tensor", lambda e, kc=kc, tt=tt: e.matmul(
                            ps_g[0:EXP, :], gw_sb[:, kc * EXP:(kc + 1) * EXP],
                            uslice(kc, tt), start=(kc == 0), stop=(kc == KC - 1)),
                            inc=(s_t if kc == KC - 1 else None))
                    P.wait("vector", s_t)
                    P.wait("vector", s_cwdst, s_cwdst.n)
                    P.op("vector", lambda e: e.tensor_copy(lsb[:], ps_g[0:EXP, :]), inc=s_v)
                    for ch in range(4):
                        c0 = ch * 128
                        P.op("tensor", lambda e, c0=c0: e.transpose(
                            ps_g[0:128, 0:EXP], lsb[:, c0:c0 + 128], identf[0:EXP, 0:EXP]),
                            waits=((s_v, s_v.n),), inc=s_t)
                        P.wait("vector", s_t)
                        P.op("vector", lambda e: e.tensor_copy(ltm[:], ps_g[0:128, 0:EXP]), inc=s_v)
                        P.selfsync("vector", s_v)
                        P.op("vector", lambda e: e.tensor_reduce(
                            mxc[:, 0:1], ltm[:], AX.X, ALU.max))
                        P.selfsync("vector", s_v)
                        P.op("vector", lambda e: e.tensor_scalar(
                            mxc[:, 1:2], mxc[:, 0:1], -1.0, None, ALU.mult), inc=s_v)
                        P.wait("scalar", s_v)
                        P.op("scalar", lambda e: e.activation(
                            exm[:], ltm[:], AF.Exp, bias=mxc[:, 1:2]), inc=s_a)
                        P.op("vector", lambda e: e.memset(cnt[:], 0.0))
                        P.selfsync("vector", s_v)
                        for j in range(EXP):
                            ljb = ap_of(ltm, j, [(0, EXP)])
                            P.op("vector", lambda e, ljb=ljb: e.tensor_tensor(
                                tm4[:], ltm[:], ljb, ALU.is_lt))
                            P.selfsync("vector", s_v)
                            P.op("vector", lambda e: e.tensor_tensor(
                                cnt[:], cnt[:], tm4[:], ALU.add))
                            P.selfsync("vector", s_v)
                            P.op("vector", lambda e, ljb=ljb: e.tensor_tensor(
                                tm4[:], ltm[:], ljb, ALU.is_equal))
                            P.selfsync("vector", s_v)
                            P.op("vector", lambda e, j=j: e.tensor_tensor(
                                tm4[:], tm4[:], gmask[:, j * EXP:(j + 1) * EXP], ALU.mult))
                            P.selfsync("vector", s_v)
                            P.op("vector", lambda e: e.tensor_tensor(
                                cnt[:], cnt[:], tm4[:], ALU.add))
                            P.selfsync("vector", s_v)
                        P.op("vector", lambda e: e.tensor_scalar(
                            cnt[:], cnt[:], 2.0, None, ALU.is_lt), inc=s_v)
                        P.selfsync("vector", s_v)
                        P.wait("vector", s_a)
                        P.op("vector", lambda e: e.tensor_tensor(
                            exm[:], exm[:], cnt[:], ALU.mult))
                        P.selfsync("vector", s_v)
                        P.op("vector", lambda e: e.tensor_reduce(
                            mxc[:, 0:1], exm[:], AX.X, ALU.add))
                        P.selfsync("vector", s_v)
                        P.op("vector", lambda e: e.reciprocal(mxc[:, 1:2], mxc[:, 0:1]))
                        P.selfsync("vector", s_v)
                        P.op("vector", lambda e: e.tensor_scalar(
                            exm[:], exm[:], mxc[:, 1:2], None, ALU.mult), inc=s_v)
                        P.selfsync("vector", s_v)
                        for ee in range(EXP):
                            P.op("tensor", lambda e, ee=ee: e.transpose(
                                ps_g[0:1, 0:128], exm[:, ee:ee + 1], identf[:]),
                                waits=((s_v, s_v.n),), inc=s_t)
                            P.wait("vector", s_t)
                            P.op("vector", lambda e, ee=ee, c0=c0: e.tensor_copy(
                                rowb[ee][:, c0:c0 + 128], ps_g[0:1, 0:128]), inc=s_v)
                    P.wait("scalar", s_v)
                    for ee in range(EXP):
                        P.op("scalar", lambda e, ee=ee, tt=tt: e.dma_start(
                            out=cwd[ee, tt * TTS:(tt + 1) * TTS].rearrange(
                                "(o n) -> o n", o=1),
                            in_=rowb[ee][:]), inc=s_cwdst, inc_k=16)

                barrier()
                P.flush()

                w13b = [fsb(f"w13_{i}", [128, 2 * KC * 128], BF16) for i in range(2)]
                silb = [fsb(f"sil{i}", [128, TTS]) for i in range(2)]
                ht32 = [fsb(f"ht{i}", [128, TTS]) for i in range(2)]
                hhb = [fsb(f"hh{i}", [128, TTS], BF16) for i in range(2)]
                cwb = fsb("cwb", [128, TOK])
                w2cst = fsb("w2cst", [128, HC * 128], BF16) if precast else None
                ps_w1 = [fes.enter_context(nc.psum_tensor(tag + f"pw1{i}", [128, TTS], F32))
                         for i in range(2)]
                ps_w3 = [fes.enter_context(nc.psum_tensor(tag + f"pw3{i}", [128, TTS], F32))
                         for i in range(2)]

                w13_used = [0, 0]
                pair_free = [0, 0]
                hh_free = [0, 0]
                cast_i = [0]
                it = 0
                for ee in range(EXP):
                    cww = ((s_v, s_v.n),) if ee > 0 else ()
                    for tt in range(NTT):
                        P.op("gpsimd", lambda e, ee=ee, tt=tt: e.dma_start(
                            out=cwb[:, tt * TTS:(tt + 1) * TTS].rearrange(
                                "p (o n) -> p o n", o=1),
                            in_=cwd[ee, tt * TTS:(tt + 1) * TTS].rearrange(
                                "(o n) -> o n", o=1).partition_broadcast(128)),
                            waits=(cww if tt == 0 else ()), inc=s_cwb, inc_k=16)
                    cw_done = s_cwb.n
                    for hc in range(HC):
                        slot = (ee * HC + hc) % 2
                        ww = ((s_t, w13_used[slot]),) if w13_used[slot] else ()
                        for w in range(2):
                            P.op("gpsimd", lambda e, w=w, ee=ee, hc=hc, slot=slot: e.dma_start(
                                out=w13b[slot][:, w * KC * 128:(w + 1) * KC * 128].rearrange(
                                    "p (k f) -> p k f", f=128),
                                in_=w13_h[w, ee, hc]),
                                waits=(ww if w == 0 else ()), inc=s_w13[slot], inc_k=16)
                        ld_done = (slot, s_w13[slot].n)
                        if precast and cast_i[0] < len(w2cast_jobs) and (hc % 2 == 0):
                            ce, cd = w2cast_jobs[cast_i[0]]
                            cast_i[0] += 1
                            P.op("gpsimd", lambda e, ce=ce, cd=cd: e.dma_start(
                                out=w2cst[:], in_=w2_h[ce, cd]), inc=s_pc, inc_k=16)
                            P.wait("gpsimd", s_pc)
                            P.op("gpsimd", lambda e, ce=ce, cd=cd: e.dma_start(
                                out=w2b[ce, cd], in_=w2cst[:]), inc=s_pc, inc_k=16)
                            P.wait("gpsimd", s_pc)
                        for tt in range(NTT):
                            pi = it % 2
                            waits = [(s_w13[ld_done[0]], ld_done[1]), (s_cwb, cw_done)]
                            if pair_free[pi]:
                                waits.append((s_v, pair_free[pi]))
                            for kc in range(KC):
                                P.op("tensor", lambda e, kc=kc, slot=slot, pi=pi, tt=tt: e.matmul(
                                    ps_w1[pi][:], w13b[slot][:, kc * 128:(kc + 1) * 128],
                                    uslice(kc, tt), start=(kc == 0), stop=(kc == KC - 1)),
                                    waits=(tuple(waits) if kc == 0 else ()))
                            for kc in range(KC):
                                P.op("tensor", lambda e, kc=kc, slot=slot, pi=pi, tt=tt: e.matmul(
                                    ps_w3[pi][:], w13b[slot][:, (KC + kc) * 128:(KC + kc + 1) * 128],
                                    uslice(kc, tt), start=(kc == 0), stop=(kc == KC - 1)),
                                    inc=(s_t if kc == KC - 1 else None))
                            mm_done = s_t.n
                            if tt == NTT - 1:
                                w13_used[slot] = mm_done
                            P.op("scalar", lambda e, pi=pi: e.activation(
                                silb[pi][:], ps_w1[pi][:], AF.Silu),
                                waits=((s_t, mm_done),), inc=s_a)
                            hslot = it % 2
                            wv = [(s_a, s_a.n)]
                            if hh_free[hslot]:
                                wv.append((s_hst[hslot], hh_free[hslot]))
                            P.op("vector", lambda e, pi=pi: e.tensor_tensor(
                                ht32[pi][:], silb[pi][:], ps_w3[pi][:], ALU.mult),
                                waits=tuple(wv), inc=s_v)
                            P.wait("vector", s_v)
                            P.op("vector", lambda e, pi=pi, hslot=hslot, tt=tt: e.tensor_tensor(
                                hhb[hslot][:], ht32[pi][:],
                                cwb[:, tt * TTS:(tt + 1) * TTS], ALU.mult), inc=s_v)
                            pair_free[pi] = s_v.n
                            P.op("scalar", lambda e, ee=ee, tt=tt, hc=hc, hslot=hslot: e.dma_start(
                                out=hds[ee, tt, :, hc, :], in_=hhb[hslot][:]),
                                waits=((s_v, s_v.n),), inc=s_hst[hslot], inc_k=16)
                            hh_free[hslot] = s_hst[hslot].n
                            it += 1
                while precast and cast_i[0] < len(w2cast_jobs):
                    ce, cd = w2cast_jobs[cast_i[0]]
                    cast_i[0] += 1
                    P.op("gpsimd", lambda e, ce=ce, cd=cd: e.dma_start(
                        out=w2cst[:], in_=w2_h[ce, cd]), inc=s_pc, inc_k=16)
                    P.wait("gpsimd", s_pc)
                    P.op("gpsimd", lambda e, ce=ce, cd=cd: e.dma_start(
                        out=w2b[ce, cd], in_=w2cst[:]), inc=s_pc, inc_k=16)
                P.wait("gpsimd", s_pc)
                barrier()
                with nc.named_scope(tag + "_w1"):
                    P.flush()

            with ExitStack() as fes:
                def fsb(name, shape, dt=F32):
                    return fes.enter_context(nc.sbuf_tensor(tag + "B" + name, shape, dt))

                hrd = [fsb(f"hrd{i}", [128, HC * TTS], BF16) for i in range(2)]
                w2s = [fsb(f"w2s{i}", [128, HC * 128], BF16) for i in range(2)]
                y_acc = fsb("yac", [128, KC * TTS])
                xt2 = fsb("xt2", [128, KC * TTS])
                x1t = fsb("x1t", [128, KC * TTS])
                ps_y = [fes.enter_context(nc.psum_tensor(tag + f"py{i}", [128, TTS], F32))
                        for i in range(2)]

                hr_used = [0, 0]
                w2_used = [0, 0]
                py_free = [0, 0]
                x1_free = 0
                jt = 0
                for tt in range(NTT):
                    b, t0 = tt // 4, (tt % 4) * TTS
                    for ee in range(EXP):
                        hs = (tt * EXP + ee) % 2
                        hw = ((s_t, hr_used[hs]),) if hr_used[hs] else ()
                        HH = HC // 2
                        P.op("sync", lambda e, ee=ee, tt=tt, hs=hs, HH=HH: e.dma_start(
                            out=hrd[hs][:, 0:HH * TTS].rearrange("p (h f) -> p h f", f=TTS),
                            in_=hds[ee, tt, :, 0:HH]), waits=hw, inc=s_hrd[hs], inc_k=16)
                        P.op("scalar", lambda e, ee=ee, tt=tt, hs=hs, HH=HH: e.dma_start(
                            out=hrd[hs][:, HH * TTS:].rearrange("p (h f) -> p h f", f=TTS),
                            in_=hds[ee, tt, :, HH:]), waits=hw, inc=s_hrd[hs], inc_k=16)
                        h_done = s_hrd[hs].n
                        for dc in range(KC):
                            ws = jt % 2
                            w2w = ((s_t, w2_used[ws]),) if w2_used[ws] else ()
                            P.op("gpsimd", lambda e, ee=ee, dc=dc, ws=ws: e.dma_start(
                                out=w2s[ws][:], in_=w2b[ee, dc]), waits=w2w,
                                inc=s_w2s[ws], inc_k=16)
                            w_done = s_w2s[ws].n
                            pyi = jt % 2
                            waits = [(s_hrd[hs], h_done), (s_w2s[ws], w_done)]
                            if py_free[pyi]:
                                waits.append((s_v, py_free[pyi]))
                            for hc in range(HC):
                                P.op("tensor", lambda e, hc=hc, ws=ws, hs=hs, pyi=pyi: e.matmul(
                                    ps_y[pyi][:], w2s[ws][:, hc * 128:(hc + 1) * 128],
                                    hrd[hs][:, hc * TTS:(hc + 1) * TTS],
                                    start=(hc == 0), stop=(hc == HC - 1)),
                                    waits=(tuple(waits) if hc == 0 else ()),
                                    inc=(s_t if hc == HC - 1 else None))
                            w2_used[ws] = s_t.n
                            if dc == KC - 1:
                                hr_used[hs] = s_t.n
                            P.wait("vector", s_t)
                            if ee == 0:
                                P.op("vector", lambda e, dc=dc, pyi=pyi: e.tensor_copy(
                                    y_acc[:, dc * TTS:(dc + 1) * TTS], ps_y[pyi][:]),
                                    inc=s_v)
                            else:
                                P.op("vector", lambda e, dc=dc, pyi=pyi: e.tensor_tensor(
                                    y_acc[:, dc * TTS:(dc + 1) * TTS],
                                    y_acc[:, dc * TTS:(dc + 1) * TTS],
                                    ps_y[pyi][:], ALU.add), inc=s_v)
                            py_free[pyi] = s_v.n
                            jt += 1
                    xw = [(s_x1st, x1_free)] if x1_free else []
                    for kc in range(KC):
                        P.op("sync", lambda e, kc=kc, b=b, t0=t0: e.dma_start(
                            out=xt2[:, kc * TTS:(kc + 1) * TTS],
                            in_=src_x[kc * 128:(kc + 1) * 128, b, t0:t0 + TTS]),
                            waits=(tuple(xw) if kc == 0 else ()), inc=s_xt, inc_k=16)
                    P.wait("vector", s_xt)
                    P.wait("vector", s_x1st, s_x1st.n)
                    for kc in range(KC):
                        P.op("vector", lambda e, kc=kc: e.scalar_tensor_tensor(
                            x1t[:, kc * TTS:(kc + 1) * TTS],
                            y_acc[:, kc * TTS:(kc + 1) * TTS], 0.5,
                            xt2[:, kc * TTS:(kc + 1) * TTS], ALU.mult, ALU.add),
                            inc=(s_v if kc == KC - 1 else None))
                    P.wait("scalar", s_v)
                    for kc in range(KC):
                        P.op("scalar", lambda e, kc=kc, b=b, t0=t0: e.dma_start(
                            out=dst_x[kc * 128:(kc + 1) * 128, b, t0:t0 + TTS],
                            in_=x1t[:, kc * TTS:(kc + 1) * TTS]), inc=s_x1st, inc_k=16)
                    x1_free = s_x1st.n

                barrier()
                with nc.named_scope(tag + "_w2"):
                    P.flush()

        def emit_scan(dst_dram):
            with ExitStack() as ses:
                def zb(name, shape, dt=F32):
                    return ses.enter_context(nc.sbuf_tensor("z" + name, shape, dt))

                CW = BC * CT
                wdt_sb = zb("wdt", [128, KC * D], BF16)
                x1c = [zb(f"x1c{i}", [128, KC * CW]) for i in range(2)]
                u2c = [zb(f"u2c{i}", [128, KC * CW]) for i in range(2)]
                u2b = [zb(f"u2b{i}", [128, KC * CW], BF16) for i in range(2)]
                dtc = [zb(f"dtc{i}", [128, KC * CW]) for i in range(2)]
                dxc = [zb(f"dxc{i}", [128, KC * CW]) for i in range(2)]
                dac = [zb(f"dac{i}", [128, CT * 256]) for i in range(2)]
                bbt = [zb(f"bb{i}", [128, CW * NST]) for i in range(2)]
                cbt = [zb(f"cb{i}", [128, CW * NST]) for i in range(2)]
                ybf = [zb(f"yb{i}", [128, KC * CW]) for i in range(2)]
                x2c = [zb(f"x2c{i}", [128, KC * CW]) for i in range(2)]
                sch = [zb(f"sch{i}", [128, CW]) for i in range(2)]
                btc = zb("btc", [NST, CW])
                btm = zb("btm", [128, NST])
                muc = zb("muc", [128, CW])
                jk2 = zb("jk2", [128, KC * CW])
                rsc = zb("rsc", [128, CW])
                jkc = zb("jkc", [128, CW])
                gcur = zb("gcur", [128, 256])
                pfold = zb("pfold", [128, 256])
                jnk2 = zb("jnk2", [128, 256])
                ztl = zb("ztl", [128, 4 * 256])
                hqt = zb("hqt", [128, 256])
                scur = zb("scur", [128, 2])
                rsum = zb("rsum", [128, 2])
                svv = zb("svv", [128, 2])
                ps_l1 = ses.enter_context(nc.psum_tensor("zpl1", [128, 512], F32))
                ps_l2 = ses.enter_context(nc.psum_tensor("zpl2", [128, 512], F32))
                ps_dt = [ses.enter_context(nc.psum_tensor(f"zpd{i}", [128, 512], F32))
                         for i in range(4)]
                ps_bc = ses.enter_context(nc.psum_tensor("zpb", [128, 512], F32))
                ps_nr = ses.enter_context(nc.psum_tensor("zpn", [128, 512], F32))

                for kc in range(KC):
                    P.op("gpsimd", lambda e, kc=kc: e.dma_start(
                        out=wdt_sb[:, kc * D:(kc + 1) * D],
                        in_=wdt_h[kc * 128:(kc + 1) * 128, :]), inc=s_ldw, inc_k=16)
                wdt_done = s_ldw.n
                P.op("vector", lambda e: e.memset(gcur[:], 0.0))
                P.op("vector", lambda e: e.memset(pfold[:], 0.0))
                P.op("vector", lambda e: e.memset(scur[:], 1.0), inc=s_v)

                prep_ready = {}
                bc_ready = {}
                step_done = {}
                step_g = {}

                def gen_prep(c):
                    pp = c % 2
                    xw = ((s_v, s_v.n),)
                    for kc in range(KC):
                        P.op("sync", lambda e, kc=kc, c=c, pp=pp: e.dma_start(
                            out=x1c[pp][:, kc * CW:(kc + 1) * CW].rearrange(
                                "p (b t) -> p b t", b=BC),
                            in_=x1s[kc * 128:(kc + 1) * 128, :, c * CT:(c + 1) * CT]),
                            waits=(xw if kc == 0 else ()), inc=s_x1c, inc_k=16)
                    x_done = s_x1c.n
                    P.wait("tensor", s_x1c, x_done)
                    P.wait("tensor", s_v, s_v.n)
                    for kc in range(KC):
                        P.op("tensor", lambda e, kc=kc, pp=pp: e.matmul(
                            ps_l1[:, 0:CW], ones_f[:], x1c[pp][:, kc * CW:(kc + 1) * CW],
                            start=(kc == 0), stop=(kc == KC - 1)),
                            inc=(s_t if kc == KC - 1 else None))
                    sq_inc = [0] * KC
                    P.wait("scalar", s_x1c, x_done)
                    for kc in range(4):
                        P.op("scalar", lambda e, kc=kc, pp=pp: e.activation(
                            sqb2[kc][:], x1c[pp][:, kc * CW:(kc + 1) * CW], AF.Square),
                            inc=s_a)
                        sq_inc[kc] = s_a.n
                    yield
                    P.wait("tensor", s_a, sq_inc[3])
                    P.wait("tensor", s_v, s_v.n)
                    for kc in range(4):
                        P.op("tensor", lambda e, kc=kc: e.matmul(
                            ps_l2[:, 0:CW], ones_f[:], sqb2[kc][:],
                            start=(kc == 0), stop=False))
                    for kc in range(4, 8):
                        P.op("scalar", lambda e, kc=kc, pp=pp: e.activation(
                            sqb2[kc][:], x1c[pp][:, kc * CW:(kc + 1) * CW], AF.Square),
                            inc=s_a)
                        sq_inc[kc] = s_a.n
                    yield
                    P.wait("tensor", s_a, sq_inc[7])
                    for kc in range(4, 8):
                        P.op("tensor", lambda e, kc=kc: e.matmul(
                            ps_l2[:, 0:CW], ones_f[:], sqb2[kc][:],
                            start=False, stop=(kc == KC - 1)),
                            inc=(s_t if kc == KC - 1 else None))
                    l2_done = s_t.n
                    P.wait("vector", s_t, l2_done)
                    P.op("vector", lambda e: e.tensor_scalar(muc[:], ps_l1[:, 0:CW], 1.0 / D, None, ALU.mult))
                    P.op("vector", lambda e: e.tensor_scalar(rsc[:], ps_l2[:, 0:CW], 1.0 / D, None, ALU.mult))
                    P.op("vector", lambda e: e.tensor_tensor(jkc[:], muc[:], muc[:], ALU.mult))
                    P.selfsync("vector", s_v)
                    P.op("vector", lambda e: e.tensor_tensor(rsc[:], rsc[:], jkc[:], ALU.subtract),
                         inc=s_v)
                    var_done = s_v.n
                    yield
                    P.wait("scalar", s_v, var_done)
                    P.op("scalar", lambda e: e.activation(rsc[:], rsc[:], AF.Ln, bias=cst[:, 2:3]),
                         inc=s_a)
                    P.wait("scalar", s_a)
                    P.op("scalar", lambda e: e.activation(rsc[:], rsc[:], AF.Exp, scale=cst[:, 1:2]),
                         inc=s_a)
                    rstd_done = s_a.n
                    P.wait("vector", s_a, rstd_done)
                    for kc in range(KC):
                        sl = slice(kc * CW, (kc + 1) * CW)
                        P.op("vector", lambda e, sl=sl, pp=pp: e.tensor_tensor(
                            u2c[pp][:, sl], x1c[pp][:, sl], muc[:], ALU.subtract))
                    P.selfsync("vector", s_v)
                    yield
                    for kc in range(KC):
                        sl = slice(kc * CW, (kc + 1) * CW)
                        P.op("vector", lambda e, sl=sl, pp=pp: e.tensor_tensor(
                            u2c[pp][:, sl], u2c[pp][:, sl], rsc[:], ALU.mult))
                    P.op("vector", lambda e, pp=pp: e.tensor_copy(u2b[pp][:], u2c[pp][:]),
                         inc=s_v)
                    u2_done = s_v.n
                    yield
                    dt_mm_done = {}
                    ev_inc = {}

                    def dt_group(dc):
                        pd = dc % 4
                        pw = ((s_v, ev_inc[dc - 4]),) if dc >= 4 else ((s_v, u2_done),)
                        for kc in range(KC):
                            P.op("tensor", lambda e, kc=kc, dc=dc, pd=pd, pp=pp: e.matmul(
                                ps_dt[pd][:, 0:CW],
                                wdt_sb[:, kc * D + dc * 128: kc * D + dc * 128 + 128],
                                u2b[pp][:, kc * CW:(kc + 1) * CW],
                                start=(kc == 0), stop=(kc == KC - 1)),
                                waits=(pw if kc == 0 else ()),
                                inc=(s_t if kc == KC - 1 else None))
                        dt_mm_done[dc] = s_t.n

                    def dt_evac(dc):
                        sl = slice(dc * CW, (dc + 1) * CW)
                        P.op("vector", lambda e, sl=sl, pd=dc % 4, dc=dc, pp=pp: e.tensor_scalar(
                            dtc[pp][:, sl], ps_dt[pd][:, 0:CW], bdt_sb[:, dc:dc + 1], None, ALU.add),
                            waits=((s_t, dt_mm_done[dc]),), inc=s_v)
                        ev_inc[dc] = s_v.n

                    dt_group(0)
                    dt_group(1)
                    yield
                    dt_group(2)
                    dt_group(3)
                    dt_evac(0)
                    dt_evac(1)
                    yield
                    dt_group(4)
                    dt_group(5)
                    dt_evac(2)
                    dt_evac(3)
                    yield
                    dt_group(6)
                    dt_group(7)
                    dt_evac(4)
                    dt_evac(5)
                    yield
                    dt_evac(6)
                    dt_evac(7)
                    P.selfsync("vector", s_v)
                    P.op("vector", lambda e, pp=pp: e.tensor_scalar(
                        jk2[:], dtc[pp][:], 0.0, None, ALU.max), inc=s_v)
                    jk_done = s_v.n
                    yield
                    P.wait("scalar", s_v, jk_done)
                    P.op("scalar", lambda e, pp=pp: e.activation(
                        dtc[pp][:], dtc[pp][:], AF.Abs), inc=s_a)
                    P.wait("scalar", s_a)
                    P.op("scalar", lambda e, pp=pp: e.activation(
                        dtc[pp][:], dtc[pp][:], AF.Exp, scale=-1.0), inc=s_a)
                    P.wait("scalar", s_a)
                    P.op("scalar", lambda e, pp=pp: e.activation(
                        dtc[pp][:], dtc[pp][:], AF.Ln, bias=cst[:, 3:4]), inc=s_a)
                    ln_done = s_a.n
                    yield
                    P.wait("vector", s_a, ln_done)
                    P.op("vector", lambda e, pp=pp: e.tensor_tensor(
                        dtc[pp][:], dtc[pp][:], jk2[:], ALU.add), inc=s_v)
                    sp_done = s_v.n
                    yield
                    P.wait("scalar", s_v, sp_done)
                    P.op("scalar", lambda e, pp=pp: e.activation(
                        dtc[pp][:], dtc[pp][:], AF.Exp, scale=SC_SCALE, bias=cst[:, 6:7]), inc=s_a)
                    P.wait("scalar", s_a)
                    P.op("scalar", lambda e, pp=pp: e.activation(
                        dtc[pp][:], dtc[pp][:], AF.Ln, bias=cst[:, 3:4]), inc=s_a)
                    P.wait("scalar", s_a)
                    P.op("scalar", lambda e, pp=pp: e.activation(
                        dtc[pp][:], dtc[pp][:], AF.Exp, scale=-1.0), inc=s_a)
                    ch_done = s_a.n
                    yield
                    P.wait("vector", s_a, ch_done)
                    P.op("vector", lambda e, pp=pp: e.tensor_scalar(
                        dtc[pp][:], dtc[pp][:], -4.0 * SC_R,
                        2.0 * SC_R + SC_C, ALU.mult, ALU.add), inc=s_v)
                    P.wait("vector", s_v)
                    P.op("vector", lambda e, pp=pp: e.tensor_tensor(
                        dxc[pp][:], dtc[pp][:], u2c[pp][:], ALU.mult), inc=s_v)
                    yield
                    prev_btm = [None]

                    def bc_mm(ib):
                        wsb = wb_sb if ib == 0 else wc_sb
                        w = ((s_v, s_v.n),) if ib == 0 else ((s_v, prev_btm[0]),)
                        for kc in range(KC):
                            P.op("tensor", lambda e, kc=kc, wsb=wsb, pp=pp: e.matmul(
                                ps_bc[0:NST, 0:CW], wsb[:, kc * NST:(kc + 1) * NST],
                                u2b[pp][:, kc * CW:(kc + 1) * CW],
                                start=(kc == 0), stop=(kc == KC - 1)),
                                waits=(w if kc == 0 else ()),
                                inc=(s_t if kc == KC - 1 else None))
                        return s_t.n

                    def bc_copy1(done):
                        P.op("vector", lambda e: e.tensor_copy(btc[:], ps_bc[0:NST, 0:CW]),
                             waits=((s_t, done),), inc=s_v)
                        return s_v.n

                    def bc_tr(done):
                        P.op("tensor", lambda e: e.transpose(
                            ps_bc[0:CW, 0:NST], btc[:], identf[0:NST, 0:NST]),
                            waits=((s_v, done),), inc=s_t)
                        return s_t.n

                    def bc_store(ib, done):
                        P.op("vector", lambda e: e.tensor_copy(
                            btm[0:CW, :], ps_bc[0:CW, 0:NST]),
                            waits=((s_t, done), (s_bst, s_bst.n)), inc=s_v)
                        prev_btm[0] = s_v.n
                        P.wait("scalar", s_v)
                        P.op("scalar", lambda e, c=c, ib=ib: e.dma_start(
                            out=bcd[ib, c].rearrange("(q n) -> q n", n=NST),
                            in_=btm[0:CW, :]), inc=s_bst, inc_k=16)
                        return s_bst.n

                    def bc_bcast(ib, done):
                        dstt = bbt[pp] if ib == 0 else cbt[pp]
                        P.wait("gpsimd", s_bst, done)
                        if c >= 2 and (c - 2) in step_done:
                            P.wait("gpsimd", s_v, step_done[c - 2])
                        P.op("gpsimd", lambda e, c=c, ib=ib, dstt=dstt: e.dma_start(
                            out=dstt[:].rearrange("p (o n) -> p o n", o=1),
                            in_=bcd[ib, c].rearrange("(o n) -> o n", o=1)
                            .partition_broadcast(128)), inc=s_bc[c % 2], inc_k=16)

                    d1 = bc_mm(0)
                    yield
                    d2 = bc_copy1(d1)
                    yield
                    d3 = bc_tr(d2)
                    yield
                    d4 = bc_store(0, d3)
                    yield
                    bc_bcast(0, d4)
                    d1 = bc_mm(1)
                    yield
                    d2 = bc_copy1(d1)
                    yield
                    d3 = bc_tr(d2)
                    yield
                    d4 = bc_store(1, d3)
                    yield
                    bc_bcast(1, d4)
                    bc_ready[c] = s_bc[c % 2].n
                    P.wait("vector", s_v, s_v.n)
                    bld = [0, 0, 0, 0]
                    PIECE = CT * 256 // 4
                    da_inc = [[0] * 4 for _ in range(4)]
                    da_params = [
                        dict(func=AF.Exp, scale=-0.125),
                        dict(func=AF.Ln, bias=cst[:, 3:4]),
                        dict(func=AF.Exp, scale=-1.0),
                        dict(func=AF.Exp, scale=cst[:, 5:6], bias=cst[:, 4:5]),
                    ]

                    def build4(base):
                        for st_ in range(base, base + 4):
                            dt_ap = ap_of(dtc[pp], st_, [(CT, BC), (CW, KC), (0, NST)])
                            pa_ap = ap_of(pa_sb, 0, [(0, BC), (NST, KC), (1, NST)])
                            P.op("vector", lambda e, st_=st_, dt_ap=dt_ap, pa_ap=pa_ap, pp=pp:
                                 e.tensor_tensor(
                                     dac[pp][:, st_ * 256:(st_ + 1) * 256].rearrange(
                                         "p (b k n) -> p b k n", b=BC, k=KC),
                                     dt_ap, pa_ap, ALU.mult),
                                 inc=(s_v if st_ % 8 == 7 else None))
                            if st_ % 8 == 7:
                                bld[st_ // 8] = s_v.n

                    def da_piece(p, k):
                        if p == 0:
                            w = ((s_v, bld[k]),)
                        else:
                            w = ((s_a, da_inc[p - 1][k]),)
                        kw = da_params[p]
                        P.op("scalar", lambda e, k=k, kw=kw, pp=pp: e.activation(
                            dac[pp][:, k * PIECE:(k + 1) * PIECE],
                            dac[pp][:, k * PIECE:(k + 1) * PIECE], **kw),
                            waits=w, inc=s_a)
                        da_inc[p][k] = s_a.n

                    for i_ in range(8):
                        build4(4 * i_)
                        yield
                    order = ([(0, k) for k in range(4)] + [(1, k) for k in range(4)]
                             + [(2, k) for k in range(4)] + [(3, k) for k in range(4)])
                    for j_, (p, k) in enumerate(order):
                        da_piece(p, k)
                        if j_ < len(order) - 1:
                            yield
                    prep_ready[c] = s_a.n

                def emit_steps(c, bg=None):
                    pp = c % 2
                    P.wait("vector", s_a, prep_ready[c])
                    P.wait("gpsimd", s_a, prep_ready[c])
                    P.wait("gpsimd", s_bc[c % 2], bc_ready[c])
                    z_inc = {}
                    hq_inc = {-1: s_g.n}
                    gv = {-4: s_v.n, -3: s_v.n, -2: s_v.n, -1: s_v.n}

                    def zsl(t):
                        i = t % 4
                        return ztl[:, i * 256:(i + 1) * 256]

                    def emit_z(t):
                        bb_ap = ap_of(bbt[pp], t * NST, [(CT * NST, BC), (0, KC), (1, NST)])
                        dx_ap = ap_of(dxc[pp], t, [(CT, BC), (CW, KC), (0, NST)])
                        w = ((s_v, gv[t - 4]),) if (t - 4) in gv else ()
                        P.op("gpsimd", lambda e, t=t, dx_ap=dx_ap, bb_ap=bb_ap: e.tensor_tensor(
                            zsl(t).rearrange("p (b k n) -> p b k n", b=BC, k=KC),
                            dx_ap, bb_ap, ALU.mult), waits=w, inc=s_g)
                        z_inc[t] = s_g.n

                    for t in range(min(3, CT)):
                        emit_z(t)
                    s_inc_prev = None
                    for t in range(CT):
                        if t + 3 < CT:
                            emit_z(t + 3)
                        if s_inc_prev is not None:
                            P.wait("vector", s_a, s_inc_prev)
                        gw_ = max(z_inc[t], hq_inc[t - 1])
                        for b in range(BC):
                            P.op("vector", lambda e, b=b, t=t: e.scalar_tensor_tensor(
                                gcur[:, b * 128:(b + 1) * 128],
                                pfold[:, b * 128:(b + 1) * 128], scur[:, b:b + 1],
                                zsl(t)[:, b * 128:(b + 1) * 128], ALU.mult, ALU.add),
                                waits=(((s_g, gw_),) if b == 0 else ()),
                                inc=(s_v if b == BC - 1 else None))
                        if t >= 1:
                            P.op("vector", lambda e, t=t: e.tensor_copy(
                                ap_of(sch[pp], t - 1, [(CT, BC)]), scur[:]))
                        P.wait("vector", s_v)
                        for b in range(BC):
                            P.op("vector", lambda e, b=b: e.scalar_tensor_tensor(
                                jnk2[:, b * 128:(b + 1) * 128],
                                gcur[:, b * 128:(b + 1) * 128], 1.0,
                                gcur[:, b * 128:(b + 1) * 128],
                                ALU.mult, ALU.mult, accum_out=rsum[:, b:b + 1]),
                                inc=(s_v if b == BC - 1 else None))
                        gv[t] = s_v.n
                        gvt = s_v.n
                        P.op("tensor", lambda e: e.matmul(
                            ps_nr[:, 0:2], ones_f[:], rsum[:], start=True, stop=True),
                            waits=((s_v, gvt),), inc=s_t)
                        P.wait("scalar", s_t)
                        P.op("scalar", lambda e: e.activation(svv[:], ps_nr[:, 0:2], AF.Ln),
                             inc=s_a)
                        P.wait("scalar", s_a)
                        P.op("scalar", lambda e: e.activation(
                            svv[:], svv[:], AF.Relu, scale=0.5, bias=cst[:, 7:8]), inc=s_a)
                        P.wait("scalar", s_a)
                        P.op("scalar", lambda e: e.activation(
                            scur[:], svv[:], AF.Exp, scale=-1.0), inc=s_a)
                        s_inc_prev = s_a.n
                        last_global = (c == NCH - 1) and (t == CT - 1)
                        if not last_global:
                            npp = pp if t + 1 < CT else (c + 1) % 2
                            nslot = (t + 1) % CT
                            if t + 1 == CT:
                                P.wait("gpsimd", s_a, prep_ready.get(c + 1, 0))
                            P.op("gpsimd", lambda e, npp=npp, nslot=nslot: e.tensor_tensor(
                                pfold[:],
                                dac[npp][:, nslot * 256:(nslot + 1) * 256],
                                gcur[:], ALU.mult),
                                waits=((s_v, gvt),), inc=s_g)
                        cb_ap = ap_of(cbt[pp], t * NST, [(CT * NST, BC), (0, KC), (1, NST)])
                        P.op("gpsimd", lambda e, cb_ap=cb_ap: e.tensor_tensor(
                            hqt[:].rearrange("p (b k n) -> p b k n", b=BC, k=KC),
                            gcur[:].rearrange("p (b k n) -> p b k n", b=BC, k=KC),
                            cb_ap, ALU.mult),
                            waits=(() if not last_global else ((s_v, gvt),)), inc=s_g)
                        hq_inc[t] = s_g.n
                        yb_ap = ap_of(ybf[pp], t, [(CT, BC), (CW, KC)])
                        P.op("vector", lambda e, yb_ap=yb_ap: e.tensor_reduce(
                            yb_ap, hqt[:].rearrange("p (b k n) -> p b k n", b=BC, k=KC),
                            AX.X, ALU.add), waits=((s_g, hq_inc[t]),), inc=s_v)
                        if bg is not None:
                            nadv = 2 if t < 17 else 1
                            if t == CT - 2:
                                nadv = 10000
                            for _ in range(nadv):
                                if next(bg, "END") == "END":
                                    bg = None
                                    break
                    P.wait("vector", s_a, s_inc_prev)
                    P.op("vector", lambda e: e.tensor_copy(
                        ap_of(sch[pp], CT - 1, [(CT, BC)]), scur[:]), inc=s_v)
                    step_done[c] = s_v.n
                    step_g[c] = s_g.n

                def emit_epi(c):
                    pp = c % 2
                    P.wait("vector", s_x2st, s_x2st.n)
                    P.wait("vector", s_g, step_g[c])
                    P.selfsync("vector", s_v)
                    yb_all = ap_of(ybf[pp], 0, [(CT, BC), (CW, KC), (1, CT)])
                    sc_all = ap_of(sch[pp], 0, [(CT, BC), (0, KC), (1, CT)])
                    P.op("vector", lambda e, yb_all=yb_all, sc_all=sc_all: e.tensor_tensor(
                        yb_all, yb_all, sc_all, ALU.mult))
                    for kc in range(KC):
                        sl = slice(kc * CW, (kc + 1) * CW)
                        P.op("vector", lambda e, sl=sl, kc=kc, pp=pp: e.scalar_tensor_tensor(
                            x2c[pp][:, sl], u2c[pp][:, sl], dp_sb[:, kc:kc + 1],
                            x1c[pp][:, sl], ALU.mult, ALU.add))
                    for kc in range(KC):
                        sl = slice(kc * CW, (kc + 1) * CW)
                        P.op("vector", lambda e, sl=sl, pp=pp: e.tensor_tensor(
                            x2c[pp][:, sl], x2c[pp][:, sl], ybf[pp][:, sl], ALU.add),
                            inc=(s_v if kc == KC - 1 else None))
                    P.wait("scalar", s_v)
                    for kc in range(KC):
                        P.op("scalar", lambda e, kc=kc, c=c, pp=pp: e.dma_start(
                            out=dst_dram[kc * 128:(kc + 1) * 128, :, c * CT:(c + 1) * CT],
                            in_=x2c[pp][:, kc * CW:(kc + 1) * CW].rearrange(
                                "p (b t) -> p b t", b=BC)), inc=s_x2st, inc_k=16)

                sqb2 = [zb(f"sqb2{i}", [128, CW]) for i in range(8)]
                P.wait("tensor", s_ldw, wdt_done)
                for _ in gen_prep(0):
                    pass
                with nc.named_scope("sp00x"):
                    P.flush()
                for c in range(NCH):
                    bg = gen_prep(c + 1) if c + 1 < NCH else None
                    emit_steps(c, bg)
                    if bg is not None:
                        for _ in bg:
                            pass
                    with nc.named_scope(f"ss{c:02d}"):
                        P.flush()
                    emit_epi(c)
                    with nc.named_scope(f"se{c:02d}"):
                        P.flush()
                barrier()
                with nc.named_scope("scanbar"):
                    P.flush()

        if parts in ("f1", "f1dbg"):
            emit_ffn(x_h, out_h, "f1", precast=True)
            if parts == "f1dbg":
                P.op("gpsimd", lambda e: e.dma_start(
                    out=dbg_h[0:4, 0:4096], in_=cwd[:, :]), inc=s_ldw, inc_k=16)
                P.op("gpsimd", lambda e: e.dma_start(
                    out=dbg_h[:, 128:128 + 44 * 512].rearrange("p (h f) -> p h f", f=512),
                    in_=hds[0, 0]), inc=s_ldw, inc_k=16)
                P.op("gpsimd", lambda e: e.dma_start(
                    out=dbg_h[:, 22656:22656 + 5632], in_=w2b[0, 0]), inc=s_ldw, inc_k=16)
                P.wait("gpsimd", s_ldw, s_ldw.n)
        elif parts == "f1scan":
            emit_ffn(x_h, x1s, "f1", precast=True)
            emit_scan(out_h)
        else:
            emit_ffn(x_h, x1s, "f1", precast=True)
            emit_scan(x2s)
            emit_ffn(x2s, out_h, "f2", precast=False)
        for ss_ in store_sems:
            P.wait("scalar", ss_, ss_.n)
        P.flush()

    return nc


def _stage_weights(w1, w3, w2, Wdt, WB, WC, gate_w, A_log, bdt, Dp):
    w1r = np.ascontiguousarray(
        np.asarray(w1[0:EXP], dtype=np.float32)
        .reshape(EXP, KC, 128, HC, 128).transpose(0, 3, 2, 1, 4))
    w3r = np.ascontiguousarray(
        np.asarray(w3[0:EXP], dtype=np.float32)
        .reshape(EXP, KC, 128, HC, 128).transpose(0, 3, 2, 1, 4))
    w13 = np.ascontiguousarray(np.stack([w1r, w3r], axis=0))
    w2r = np.ascontiguousarray(
        np.asarray(w2[0:EXP], dtype=np.float32)
        .reshape(EXP, HC, 128, KC, 128).transpose(0, 3, 2, 1, 4)
        .reshape(EXP, KC, 128, HC * 128))
    return dict(
        w13=w13, w2=w2r,
        wdt=np.ascontiguousarray(np.asarray(Wdt, dtype=np.float32).T),
        wb=np.ascontiguousarray(np.asarray(WB, dtype=np.float32).T),
        wc=np.ascontiguousarray(np.asarray(WC, dtype=np.float32).T),
        gw=np.ascontiguousarray(np.asarray(gate_w[0:EXP], dtype=np.float32).T),
        alog=np.ascontiguousarray(np.asarray(A_log, dtype=np.float32)),
        bdt=np.ascontiguousarray(np.asarray(bdt, dtype=np.float32).reshape(KC, 128).T),
        dp=np.ascontiguousarray(np.asarray(Dp, dtype=np.float32).reshape(KC, 128).T),
    )


def kernel(L0_attn_x, L0_attn_Wdt, L0_attn_bdt, L0_attn_WB, L0_attn_WC,
           L0_attn_A_log, L0_attn_Dp, L0_ffn_gate_w, L0_ffn_w1, L0_ffn_w2,
           L0_ffn_w3):
    ncores = 8
    x = np.asarray(L0_attn_x, dtype=np.float32)
    shared = _stage_weights(L0_ffn_w1, L0_ffn_w3, L0_ffn_w2, L0_attn_Wdt,
                            L0_attn_WB, L0_attn_WC, L0_ffn_gate_w,
                            L0_attn_A_log, L0_attn_bdt, L0_attn_Dp)
    in_maps = []
    for c in range(ncores):
        m = {"x": np.ascontiguousarray(x[2 * c:2 * c + 2].transpose(2, 0, 1))}
        m.update(shared)
        in_maps.append(m)
    nc = build_program()
    res = run_bass_kernel_spmd(nc, in_maps, core_ids=list(range(ncores)))
    outs = [np.asarray(res.results[c]["out"]).transpose(1, 2, 0)
            for c in range(ncores)]
    return np.ascontiguousarray(np.concatenate(outs, axis=0))


# version 14 (final)
# speedup vs baseline: 1.0104x; 1.0104x over previous
import numpy as np
from contextlib import ExitStack

import concourse.bass as bass
import concourse.mybir as mybir
from concourse.bass_utils import run_bass_kernel_spmd

F32 = mybir.dt.float32
BF16 = mybir.dt.bfloat16
AF = mybir.ActivationFunctionType
ALU = mybir.AluOpType
AX = mybir.AxisListType

D = 1024
KC = 8
NST = 16
EXP = 4
H = 5632
HC = 44
BC = 2
T = 2048
TOK = BC * T
NTT = 8
TTS = 512
CT = 32
NCH = T // CT

SC_C = (0.1 + 0.001) / 2.0
SC_R = (0.1 - 0.001) / 2.0
SC_SCALE = 1.0 / (SC_R + 1e-8)
SC_BIAS = -SC_C * SC_SCALE
LN20 = float(np.log(20.0))


class Sem:
    def __init__(self, es, nc, name):
        self.h = es.enter_context(nc.semaphore(name))
        self.n = 0

    def bump(self, k=1):
        self.n += k
        return self.n


class Plan:
    ENGINES = ("sync", "gpsimd", "vector", "scalar", "tensor")

    def __init__(self, block):
        self.block = block
        self.streams = {e: [] for e in self.ENGINES}

    def op(self, eng, fn, waits=(), inc=None, inc_k=1):
        wl = [(s.h, v) for (s, v) in waits]
        tgt = None
        if inc is not None:
            inc.bump(inc_k)
            tgt = (inc.h, inc_k)

        def run(e, fn=fn, wl=wl, tgt=tgt):
            for h, v in wl:
                e.wait_ge(h, v)
            inst = fn(e)
            if tgt is not None:
                if inst is None:
                    inst = e.nop()
                inst.then_inc(*tgt)
        self.streams[eng].append(run)

    def selfsync(self, eng, sem):
        k = sem.bump()

        def run(e, h=sem.h, k=k):
            e.nop().then_inc(h, 1)
            e.wait_ge(h, k)
        self.streams[eng].append(run)

    def wait(self, eng, sem, val=None):
        v = sem.n if val is None else val
        self.streams[eng].append(lambda e, h=sem.h, v=v: e.wait_ge(h, v))

    def flush(self):
        for name in self.ENGINES:
            fns = self.streams[name]
            if not fns:
                continue
            section = getattr(self.block, name)

            def body(eng, fns=fns):
                for f in fns:
                    f(eng)
            section(body)
            self.streams[name] = []


def ap_of(tile, offset_cols, dims):
    a = tile.ap()
    part = list(a.ap[0])
    return bass.AP(a.tensor, a.offset + offset_cols,
                   [part] + [list(d) for d in dims])


def build_program(parts="all"):
    nc = bass.Bass("TRN2", target_bir_lowering=False)

    x_h = nc.dram_tensor("x", [D, BC, T], F32, kind="ExternalInput").ap()
    w13_h = nc.dram_tensor("w13", [2, EXP, HC, 128, KC, 128], F32, kind="ExternalInput").ap()
    w2_h = nc.dram_tensor("w2", [EXP, KC, 128, HC * 128], F32, kind="ExternalInput").ap()
    wdt_h = nc.dram_tensor("wdt", [D, D], F32, kind="ExternalInput").ap()
    wb_h = nc.dram_tensor("wb", [D, NST], F32, kind="ExternalInput").ap()
    wc_h = nc.dram_tensor("wc", [D, NST], F32, kind="ExternalInput").ap()
    gw_h = nc.dram_tensor("gw", [D, EXP], F32, kind="ExternalInput").ap()
    alog_h = nc.dram_tensor("alog", [D, NST], F32, kind="ExternalInput").ap()
    bdt_h = nc.dram_tensor("bdt", [128, KC], F32, kind="ExternalInput").ap()
    dp_h = nc.dram_tensor("dp", [128, KC], F32, kind="ExternalInput").ap()
    out_h = nc.dram_tensor("out", [D, BC, T], F32, kind="ExternalOutput").ap()
    dbg_h = (nc.dram_tensor("dbg", [128, 32768], F32, kind="ExternalOutput").ap()
             if parts == "f1dbg" else None)

    x1s = nc.dram_tensor("x1s", [D, BC, T], F32, kind="Internal").ap()
    x2s = nc.dram_tensor("x2s", [D, BC, T], F32, kind="Internal").ap()
    hds = nc.dram_tensor("hds", [EXP, NTT, 128, HC, TTS], BF16, kind="Internal").ap()
    w2b = nc.dram_tensor("w2b", [EXP, KC, 128, HC * 128], BF16, kind="Internal").ap()
    cwd = nc.dram_tensor("cwd", [EXP, TOK], F32, kind="Internal").ap()
    bcd = nc.dram_tensor("bcd", [2, NCH, BC * CT * NST], F32, kind="Internal").ap()

    with ExitStack() as es:
        s_ldx = Sem(es, nc, "ldx")
        s_ldw = Sem(es, nc, "ldw")
        s_st = Sem(es, nc, "st")
        s_t = Sem(es, nc, "t")
        s_v = Sem(es, nc, "v")
        s_a = Sem(es, nc, "a")
        s_g = Sem(es, nc, "g")
        s_ph = Sem(es, nc, "ph")
        s_xt = Sem(es, nc, "xt")
        s_w13 = [Sem(es, nc, f"w13_{i}") for i in range(2)]
        s_cwb = Sem(es, nc, "cwb")
        s_pc = Sem(es, nc, "pc")
        s_hrd = [Sem(es, nc, f"hrd{i}") for i in range(2)]
        s_hr2 = [Sem(es, nc, f"hr2{i}") for i in range(2)]
        s_w2s = [Sem(es, nc, f"w2s{i}") for i in range(2)]
        s_hst = [Sem(es, nc, f"hst{i}") for i in range(2)]
        s_x1st = Sem(es, nc, "x1st")
        s_cwdst = Sem(es, nc, "cwdst")
        s_x1c = Sem(es, nc, "x1c")
        s_bst = Sem(es, nc, "bst")
        s_bc = [Sem(es, nc, f"bc{i}") for i in range(2)]
        s_x2st = Sem(es, nc, "x2st")
        store_sems = [s_st, s_hst[0], s_hst[1], s_x1st, s_cwdst, s_bst, s_x2st]

        def sb(name, shape, dt=F32):
            return es.enter_context(nc.sbuf_tensor(name, shape, dt))

        ones_f = sb("ones_f", [128, 128])
        identf = sb("identf", [128, 128])
        gw_sb = sb("gw_sb", [128, KC * EXP], BF16)
        wb_sb = sb("wb_sb", [128, KC * NST], BF16)
        wc_sb = sb("wc_sb", [128, KC * NST], BF16)
        pa_sb = sb("pa_sb", [128, KC * NST])
        bdt_sb = sb("bdt_sb", [128, KC])
        dp_sb = sb("dp_sb", [128, KC])
        gmask = sb("gmask", [128, EXP * EXP])
        cst = sb("cst", [128, 8])

        block = es.enter_context(nc.Block())
        P = Plan(block)
        nbar = [0]

        def barrier():
            for ss_ in store_sems:
                P.wait("scalar", ss_, ss_.n)
            nbar[0] += 1
            for eng in Plan.ENGINES:
                P.op(eng, lambda e: e.nop(), inc=s_ph)
            tgt = s_ph.n
            for eng in Plan.ENGINES:
                P.wait(eng, s_ph, tgt)

        if True:
            P.op("gpsimd", lambda e: e.memset(ones_f[:], 1.0))
            P.op("gpsimd", lambda e: e.memset(identf[:], 1.0))
            P.op("gpsimd", lambda e: e.affine_select(
                identf[:], identf[:], [[1, 128]], ALU.is_equal, 0.0,
                base=0, channel_multiplier=-1))
            for j in range(EXP):
                for ee in range(EXP):
                    P.op("gpsimd", lambda e, j=j, ee=ee: e.memset(
                        gmask[:, j * EXP + ee:j * EXP + ee + 1], 1.0 if ee > j else 0.0))
            for i, cv in enumerate([LN20, -0.5, 1e-5, 1.0, 16.0, -32.0, SC_BIAS, -LN20]):
                P.op("gpsimd", lambda e, i=i, cv=cv: e.memset(cst[:, i:i + 1], cv))
        if True:
            P.op("gpsimd", lambda e: e.dma_start(
                out=gw_sb[:].rearrange("p (k e) -> p k e", e=EXP),
                in_=gw_h.rearrange("(k p) e -> p k e", p=128)), inc=s_ldw, inc_k=16)
            P.op("gpsimd", lambda e: e.dma_start(
                out=wb_sb[:].rearrange("p (k n) -> p k n", n=NST),
                in_=wb_h.rearrange("(k p) n -> p k n", p=128)), inc=s_ldw, inc_k=16)
            P.op("gpsimd", lambda e: e.dma_start(
                out=wc_sb[:].rearrange("p (k n) -> p k n", n=NST),
                in_=wc_h.rearrange("(k p) n -> p k n", p=128)), inc=s_ldw, inc_k=16)
            P.op("sync", lambda e: e.dma_start(
                out=pa_sb[:].rearrange("p (k n) -> p k n", n=NST),
                in_=alog_h.rearrange("(k p) n -> p k n", p=128)), inc=s_ldx, inc_k=16)
            P.op("sync", lambda e: e.dma_start(out=bdt_sb[:], in_=bdt_h), inc=s_ldx, inc_k=16)
            P.op("sync", lambda e: e.dma_start(out=dp_sb[:], in_=dp_h), inc=s_ldx, inc_k=16)
            P.wait("scalar", s_ldx)
            P.op("scalar", lambda e: e.activation(pa_sb[:], pa_sb[:], AF.Tanh, scale=0.1))
            P.selfsync("scalar", s_a)
            P.op("scalar", lambda e: e.activation(pa_sb[:], pa_sb[:], AF.Exp, scale=10.0),
                 inc=s_a)
            P.wait("tensor", s_ldw)
            P.wait("tensor", s_ldx)
        barrier()
        with nc.named_scope("p0"):
            P.flush()

        w2cast_jobs = [(ee, dc) for ee in range(EXP) for dc in range(KC)]

        def emit_ffn(src_x, dst_x, tag, precast):
            with ExitStack() as fes:
                def fsb(name, shape, dt=F32):
                    return fes.enter_context(nc.sbuf_tensor(tag + name, shape, dt))

                u_bf = fsb("u", [128, KC * TOK], BF16)
                xt = fsb("xt", [128, KC * TTS])
                sqb = [fsb(f"sq{i}", [128, TTS]) for i in range(2)]
                mu_t = fsb("mu", [128, TTS])
                rs_t = fsb("rs", [128, TTS])
                jk = fsb("jk", [128, TTS])
                lsb = fsb("lsb", [EXP, TTS])
                ltm = fsb("ltm", [128, EXP])
                exm = fsb("exm", [128, EXP])
                cnt = fsb("cnt", [128, EXP])
                tm4 = fsb("tm4", [128, EXP])
                mxc = fsb("mxc", [128, 2])
                rowb = [fsb(f"row{e}", [1, TTS]) for e in range(EXP)]
                ps_s1 = fes.enter_context(nc.psum_tensor(tag + "ps1", [128, TTS], F32))
                ps_s2 = fes.enter_context(nc.psum_tensor(tag + "ps2", [128, TTS], F32))
                ps_g = fes.enter_context(nc.psum_tensor(tag + "psg", [128, TTS], F32))

                def uslice(kc, tt):
                    return u_bf[:, kc * TOK + tt * TTS: kc * TOK + (tt + 1) * TTS]

                for tt in range(NTT):
                    b, t0 = tt // 4, (tt % 4) * TTS
                    xw = ((s_v, s_v.n),) if tt > 0 else ()
                    for kc in range(KC):
                        P.op("sync", lambda e, kc=kc, b=b, t0=t0: e.dma_start(
                            out=xt[:, kc * TTS:(kc + 1) * TTS],
                            in_=src_x[kc * 128:(kc + 1) * 128, b, t0:t0 + TTS]),
                            waits=(xw if kc == 0 else ()), inc=s_xt, inc_k=16)
                    x_done = s_xt.n
                    P.wait("tensor", s_xt, x_done)
                    if tt > 0:
                        P.wait("tensor", s_v, s_v.n)
                    for kc in range(KC):
                        P.op("tensor", lambda e, kc=kc: e.matmul(
                            ps_s1[:], ones_f[:], xt[:, kc * TTS:(kc + 1) * TTS],
                            start=(kc == 0), stop=(kc == KC - 1)),
                            inc=(s_t if kc == KC - 1 else None))
                    P.wait("scalar", s_xt, x_done)
                    for kc in range(KC):
                        w = ((s_t, s_t.n),) if kc >= 2 else ()
                        P.op("scalar", lambda e, kc=kc: e.activation(
                            sqb[kc % 2][:], xt[:, kc * TTS:(kc + 1) * TTS], AF.Square),
                            waits=w, inc=s_a)
                        P.op("tensor", lambda e, kc=kc: e.matmul(
                            ps_s2[:], ones_f[:], sqb[kc % 2][:],
                            start=(kc == 0), stop=(kc == KC - 1)),
                            waits=((s_a, s_a.n),), inc=s_t)
                    stats_done = s_t.n
                    P.wait("vector", s_t, stats_done)
                    P.op("vector", lambda e: e.tensor_scalar(
                        mu_t[:], ps_s1[:], 1.0 / D, None, ALU.mult))
                    P.op("vector", lambda e: e.tensor_scalar(
                        rs_t[:], ps_s2[:], 1.0 / D, None, ALU.mult), inc=s_v)
                    P.wait("vector", s_v)
                    P.op("vector", lambda e: e.tensor_tensor(jk[:], mu_t[:], mu_t[:], ALU.mult),
                         inc=s_v)
                    P.wait("vector", s_v)
                    P.op("vector", lambda e: e.tensor_tensor(
                        rs_t[:], rs_t[:], jk[:], ALU.subtract), inc=s_v)
                    P.wait("scalar", s_v)
                    P.op("scalar", lambda e: e.activation(rs_t[:], rs_t[:], AF.Ln, bias=cst[:, 2:3]),
                         inc=s_a)
                    P.wait("scalar", s_a)
                    P.op("scalar", lambda e: e.activation(rs_t[:], rs_t[:], AF.Exp, scale=cst[:, 1:2]),
                         inc=s_a)
                    P.wait("vector", s_a)
                    for kc in range(KC):
                        P.op("vector", lambda e, kc=kc: e.tensor_tensor(
                            jk[:], xt[:, kc * TTS:(kc + 1) * TTS], mu_t[:], ALU.subtract),
                            inc=s_v)
                        P.wait("vector", s_v)
                        P.op("vector", lambda e, kc=kc, tt=tt: e.tensor_tensor(
                            uslice(kc, tt), jk[:], rs_t[:], ALU.mult),
                            inc=(s_v if kc == KC - 1 else None))
                    u_done = s_v.n
                    P.wait("tensor", s_v, u_done)
                    for kc in range(KC):
                        P.op("tensor", lambda e, kc=kc, tt=tt: e.matmul(
                            ps_g[0:EXP, :], gw_sb[:, kc * EXP:(kc + 1) * EXP],
                            uslice(kc, tt), start=(kc == 0), stop=(kc == KC - 1)),
                            inc=(s_t if kc == KC - 1 else None))
                    P.wait("vector", s_t)
                    P.wait("vector", s_cwdst, s_cwdst.n)
                    P.op("vector", lambda e: e.tensor_copy(lsb[:], ps_g[0:EXP, :]), inc=s_v)
                    for ch in range(4):
                        c0 = ch * 128
                        P.op("tensor", lambda e, c0=c0: e.transpose(
                            ps_g[0:128, 0:EXP], lsb[:, c0:c0 + 128], identf[0:EXP, 0:EXP]),
                            waits=((s_v, s_v.n),), inc=s_t)
                        P.wait("vector", s_t)
                        P.op("vector", lambda e: e.tensor_copy(ltm[:], ps_g[0:128, 0:EXP]), inc=s_v)
                        P.selfsync("vector", s_v)
                        P.op("vector", lambda e: e.tensor_reduce(
                            mxc[:, 0:1], ltm[:], AX.X, ALU.max))
                        P.selfsync("vector", s_v)
                        P.op("vector", lambda e: e.tensor_scalar(
                            mxc[:, 1:2], mxc[:, 0:1], -1.0, None, ALU.mult), inc=s_v)
                        P.wait("scalar", s_v)
                        P.op("scalar", lambda e: e.activation(
                            exm[:], ltm[:], AF.Exp, bias=mxc[:, 1:2]), inc=s_a)
                        P.op("vector", lambda e: e.memset(cnt[:], 0.0))
                        P.selfsync("vector", s_v)
                        for j in range(EXP):
                            ljb = ap_of(ltm, j, [(0, EXP)])
                            P.op("vector", lambda e, ljb=ljb: e.tensor_tensor(
                                tm4[:], ltm[:], ljb, ALU.is_lt))
                            P.selfsync("vector", s_v)
                            P.op("vector", lambda e: e.tensor_tensor(
                                cnt[:], cnt[:], tm4[:], ALU.add))
                            P.selfsync("vector", s_v)
                            P.op("vector", lambda e, ljb=ljb: e.tensor_tensor(
                                tm4[:], ltm[:], ljb, ALU.is_equal))
                            P.selfsync("vector", s_v)
                            P.op("vector", lambda e, j=j: e.tensor_tensor(
                                tm4[:], tm4[:], gmask[:, j * EXP:(j + 1) * EXP], ALU.mult))
                            P.selfsync("vector", s_v)
                            P.op("vector", lambda e: e.tensor_tensor(
                                cnt[:], cnt[:], tm4[:], ALU.add))
                            P.selfsync("vector", s_v)
                        P.op("vector", lambda e: e.tensor_scalar(
                            cnt[:], cnt[:], 2.0, None, ALU.is_lt), inc=s_v)
                        P.selfsync("vector", s_v)
                        P.wait("vector", s_a)
                        P.op("vector", lambda e: e.tensor_tensor(
                            exm[:], exm[:], cnt[:], ALU.mult))
                        P.selfsync("vector", s_v)
                        P.op("vector", lambda e: e.tensor_reduce(
                            mxc[:, 0:1], exm[:], AX.X, ALU.add))
                        P.selfsync("vector", s_v)
                        P.op("vector", lambda e: e.reciprocal(mxc[:, 1:2], mxc[:, 0:1]))
                        P.selfsync("vector", s_v)
                        P.op("vector", lambda e: e.tensor_scalar(
                            exm[:], exm[:], mxc[:, 1:2], None, ALU.mult), inc=s_v)
                        P.selfsync("vector", s_v)
                        for ee in range(EXP):
                            P.op("tensor", lambda e, ee=ee: e.transpose(
                                ps_g[0:1, 0:128], exm[:, ee:ee + 1], identf[:]),
                                waits=((s_v, s_v.n),), inc=s_t)
                            P.wait("vector", s_t)
                            P.op("vector", lambda e, ee=ee, c0=c0: e.tensor_copy(
                                rowb[ee][:, c0:c0 + 128], ps_g[0:1, 0:128]), inc=s_v)
                    P.wait("scalar", s_v)
                    for ee in range(EXP):
                        P.op("scalar", lambda e, ee=ee, tt=tt: e.dma_start(
                            out=cwd[ee, tt * TTS:(tt + 1) * TTS].rearrange(
                                "(o n) -> o n", o=1),
                            in_=rowb[ee][:]), inc=s_cwdst, inc_k=16)

                barrier()
                P.flush()

                w13b = [fsb(f"w13_{i}", [128, 2 * KC * 128], BF16) for i in range(2)]
                silb = [fsb(f"sil{i}", [128, TTS]) for i in range(2)]
                ht32 = [fsb(f"ht{i}", [128, TTS]) for i in range(2)]
                hhb = [fsb(f"hh{i}", [128, TTS], BF16) for i in range(2)]
                cwb = fsb("cwb", [128, TOK])
                w2cst = fsb("w2cst", [128, HC * 128], BF16) if precast else None
                ps_w1 = [fes.enter_context(nc.psum_tensor(tag + f"pw1{i}", [128, TTS], F32))
                         for i in range(2)]
                ps_w3 = [fes.enter_context(nc.psum_tensor(tag + f"pw3{i}", [128, TTS], F32))
                         for i in range(2)]

                w13_used = [0, 0]
                pair_free = [0, 0]
                hh_free = [0, 0]
                cast_i = [0]
                it = 0
                for ee in range(EXP):
                    cww = ((s_v, s_v.n),) if ee > 0 else ()
                    for tt in range(NTT):
                        P.op("gpsimd", lambda e, ee=ee, tt=tt: e.dma_start(
                            out=cwb[:, tt * TTS:(tt + 1) * TTS].rearrange(
                                "p (o n) -> p o n", o=1),
                            in_=cwd[ee, tt * TTS:(tt + 1) * TTS].rearrange(
                                "(o n) -> o n", o=1).partition_broadcast(128)),
                            waits=(cww if tt == 0 else ()), inc=s_cwb, inc_k=16)
                    cw_done = s_cwb.n
                    for hc in range(HC):
                        slot = (ee * HC + hc) % 2
                        ww = ((s_t, w13_used[slot]),) if w13_used[slot] else ()
                        for w in range(2):
                            P.op("gpsimd", lambda e, w=w, ee=ee, hc=hc, slot=slot: e.dma_start(
                                out=w13b[slot][:, w * KC * 128:(w + 1) * KC * 128].rearrange(
                                    "p (k f) -> p k f", f=128),
                                in_=w13_h[w, ee, hc]),
                                waits=(ww if w == 0 else ()), inc=s_w13[slot], inc_k=16)
                        ld_done = (slot, s_w13[slot].n)
                        if precast and cast_i[0] < len(w2cast_jobs) and (hc % 2 == 0):
                            ce, cd = w2cast_jobs[cast_i[0]]
                            cast_i[0] += 1
                            P.op("gpsimd", lambda e, ce=ce, cd=cd: e.dma_start(
                                out=w2cst[:], in_=w2_h[ce, cd]), inc=s_pc, inc_k=16)
                            P.wait("gpsimd", s_pc)
                            P.op("gpsimd", lambda e, ce=ce, cd=cd: e.dma_start(
                                out=w2b[ce, cd], in_=w2cst[:]), inc=s_pc, inc_k=16)
                            P.wait("gpsimd", s_pc)
                        for tt in range(NTT):
                            pi = it % 2
                            waits = [(s_w13[ld_done[0]], ld_done[1]), (s_cwb, cw_done)]
                            if pair_free[pi]:
                                waits.append((s_v, pair_free[pi]))
                            for kc in range(KC):
                                P.op("tensor", lambda e, kc=kc, slot=slot, pi=pi, tt=tt: e.matmul(
                                    ps_w1[pi][:], w13b[slot][:, kc * 128:(kc + 1) * 128],
                                    uslice(kc, tt), start=(kc == 0), stop=(kc == KC - 1)),
                                    waits=(tuple(waits) if kc == 0 else ()))
                            for kc in range(KC):
                                P.op("tensor", lambda e, kc=kc, slot=slot, pi=pi, tt=tt: e.matmul(
                                    ps_w3[pi][:], w13b[slot][:, (KC + kc) * 128:(KC + kc + 1) * 128],
                                    uslice(kc, tt), start=(kc == 0), stop=(kc == KC - 1)),
                                    inc=(s_t if kc == KC - 1 else None))
                            mm_done = s_t.n
                            if tt == NTT - 1:
                                w13_used[slot] = mm_done
                            P.op("scalar", lambda e, pi=pi: e.activation(
                                silb[pi][:], ps_w1[pi][:], AF.Silu),
                                waits=((s_t, mm_done),), inc=s_a)
                            hslot = it % 2
                            wv = [(s_a, s_a.n)]
                            if hh_free[hslot]:
                                wv.append((s_hst[hslot], hh_free[hslot]))
                            P.op("vector", lambda e, pi=pi: e.tensor_tensor(
                                ht32[pi][:], silb[pi][:], ps_w3[pi][:], ALU.mult),
                                waits=tuple(wv), inc=s_v)
                            P.wait("vector", s_v)
                            P.op("vector", lambda e, pi=pi, hslot=hslot, tt=tt: e.tensor_tensor(
                                hhb[hslot][:], ht32[pi][:],
                                cwb[:, tt * TTS:(tt + 1) * TTS], ALU.mult), inc=s_v)
                            pair_free[pi] = s_v.n
                            P.op("scalar", lambda e, ee=ee, tt=tt, hc=hc, hslot=hslot: e.dma_start(
                                out=hds[ee, tt, :, hc, :], in_=hhb[hslot][:]),
                                waits=((s_v, s_v.n),), inc=s_hst[hslot], inc_k=16)
                            hh_free[hslot] = s_hst[hslot].n
                            it += 1
                while precast and cast_i[0] < len(w2cast_jobs):
                    ce, cd = w2cast_jobs[cast_i[0]]
                    cast_i[0] += 1
                    P.op("gpsimd", lambda e, ce=ce, cd=cd: e.dma_start(
                        out=w2cst[:], in_=w2_h[ce, cd]), inc=s_pc, inc_k=16)
                    P.wait("gpsimd", s_pc)
                    P.op("gpsimd", lambda e, ce=ce, cd=cd: e.dma_start(
                        out=w2b[ce, cd], in_=w2cst[:]), inc=s_pc, inc_k=16)
                P.wait("gpsimd", s_pc)
                barrier()
                with nc.named_scope(tag + "_w1"):
                    P.flush()

            with ExitStack() as fes:
                def fsb(name, shape, dt=F32):
                    return fes.enter_context(nc.sbuf_tensor(tag + "B" + name, shape, dt))

                hrd = [fsb(f"hrd{i}", [128, HC * TTS], BF16) for i in range(2)]
                w2s = [fsb(f"w2s{i}", [128, HC * 128], BF16) for i in range(2)]
                y_acc = fsb("yac", [128, KC * TTS])
                xt2 = fsb("xt2", [128, KC * TTS])
                x1t = fsb("x1t", [128, KC * TTS])
                ps_y = [fes.enter_context(nc.psum_tensor(tag + f"py{i}", [128, TTS], F32))
                        for i in range(2)]

                hr_used = [0, 0]
                w2_used = [0, 0]
                py_free = [0, 0]
                x1_free = 0
                jt = 0
                for tt in range(NTT):
                    b, t0 = tt // 4, (tt % 4) * TTS
                    for ee in range(EXP):
                        hs = (tt * EXP + ee) % 2
                        hw = ((s_t, hr_used[hs]),) if hr_used[hs] else ()
                        HH = HC // 2
                        P.op("sync", lambda e, ee=ee, tt=tt, hs=hs, HH=HH: e.dma_start(
                            out=hrd[hs][:, 0:HH * TTS].rearrange("p (h f) -> p h f", f=TTS),
                            in_=hds[ee, tt, :, 0:HH]), waits=hw, inc=s_hrd[hs], inc_k=16)
                        P.op("sync", lambda e, ee=ee, tt=tt, hs=hs, HH=HH: e.dma_start(
                            out=hrd[hs][:, HH * TTS:].rearrange("p (h f) -> p h f", f=TTS),
                            in_=hds[ee, tt, :, HH:]), inc=s_hr2[hs], inc_k=16)
                        h_done = s_hrd[hs].n
                        h2_done = s_hr2[hs].n
                        for dc in range(KC):
                            ws = jt % 2
                            w2w = ((s_t, w2_used[ws]),) if w2_used[ws] else ()
                            P.op("gpsimd", lambda e, ee=ee, dc=dc, ws=ws: e.dma_start(
                                out=w2s[ws][:], in_=w2b[ee, dc]), waits=w2w,
                                inc=s_w2s[ws], inc_k=16)
                            w_done = s_w2s[ws].n
                            pyi = jt % 2
                            waits = [(s_hrd[hs], h_done), (s_w2s[ws], w_done)]
                            if py_free[pyi]:
                                waits.append((s_v, py_free[pyi]))
                            for hc in range(HC):
                                if hc == 0:
                                    w = tuple(waits)
                                elif hc == HH:
                                    w = ((s_hr2[hs], h2_done),)
                                else:
                                    w = ()
                                P.op("tensor", lambda e, hc=hc, ws=ws, hs=hs, pyi=pyi: e.matmul(
                                    ps_y[pyi][:], w2s[ws][:, hc * 128:(hc + 1) * 128],
                                    hrd[hs][:, hc * TTS:(hc + 1) * TTS],
                                    start=(hc == 0), stop=(hc == HC - 1)),
                                    waits=w,
                                    inc=(s_t if hc == HC - 1 else None))
                            w2_used[ws] = s_t.n
                            if dc == KC - 1:
                                hr_used[hs] = s_t.n
                            P.wait("vector", s_t)
                            if ee == 0:
                                P.op("vector", lambda e, dc=dc, pyi=pyi: e.tensor_copy(
                                    y_acc[:, dc * TTS:(dc + 1) * TTS], ps_y[pyi][:]),
                                    inc=s_v)
                            else:
                                P.op("vector", lambda e, dc=dc, pyi=pyi: e.tensor_tensor(
                                    y_acc[:, dc * TTS:(dc + 1) * TTS],
                                    y_acc[:, dc * TTS:(dc + 1) * TTS],
                                    ps_y[pyi][:], ALU.add), inc=s_v)
                            py_free[pyi] = s_v.n
                            jt += 1
                    xw = [(s_x1st, x1_free)] if x1_free else []
                    for kc in range(KC):
                        P.op("scalar", lambda e, kc=kc, b=b, t0=t0: e.dma_start(
                            out=xt2[:, kc * TTS:(kc + 1) * TTS],
                            in_=src_x[kc * 128:(kc + 1) * 128, b, t0:t0 + TTS]),
                            waits=(tuple(xw) if kc == 0 else ()), inc=s_xt, inc_k=16)
                    P.wait("vector", s_xt)
                    P.wait("vector", s_x1st, s_x1st.n)
                    for kc in range(KC):
                        P.op("vector", lambda e, kc=kc: e.scalar_tensor_tensor(
                            x1t[:, kc * TTS:(kc + 1) * TTS],
                            y_acc[:, kc * TTS:(kc + 1) * TTS], 0.5,
                            xt2[:, kc * TTS:(kc + 1) * TTS], ALU.mult, ALU.add),
                            inc=(s_v if kc == KC - 1 else None))
                    P.wait("scalar", s_v)
                    for kc in range(KC):
                        P.op("scalar", lambda e, kc=kc, b=b, t0=t0: e.dma_start(
                            out=dst_x[kc * 128:(kc + 1) * 128, b, t0:t0 + TTS],
                            in_=x1t[:, kc * TTS:(kc + 1) * TTS]), inc=s_x1st, inc_k=16)
                    x1_free = s_x1st.n

                barrier()
                with nc.named_scope(tag + "_w2"):
                    P.flush()

        def emit_scan(dst_dram):
            with ExitStack() as ses:
                def zb(name, shape, dt=F32):
                    return ses.enter_context(nc.sbuf_tensor("z" + name, shape, dt))

                CW = BC * CT
                wdt_sb = zb("wdt", [128, KC * D], BF16)
                x1c = [zb(f"x1c{i}", [128, KC * CW]) for i in range(2)]
                u2c = [zb(f"u2c{i}", [128, KC * CW]) for i in range(2)]
                u2b = [zb(f"u2b{i}", [128, KC * CW], BF16) for i in range(2)]
                dtc = [zb(f"dtc{i}", [128, KC * CW]) for i in range(2)]
                dxc = [zb(f"dxc{i}", [128, KC * CW]) for i in range(2)]
                dac = [zb(f"dac{i}", [128, CT * 256]) for i in range(2)]
                bbt = [zb(f"bb{i}", [128, CW * NST]) for i in range(2)]
                cbt = [zb(f"cb{i}", [128, CW * NST]) for i in range(2)]
                ybf = [zb(f"yb{i}", [128, KC * CW]) for i in range(2)]
                x2c = [zb(f"x2c{i}", [128, KC * CW]) for i in range(2)]
                sch = [zb(f"sch{i}", [128, CW]) for i in range(2)]
                btc = zb("btc", [NST, CW])
                btm = zb("btm", [128, NST])
                muc = zb("muc", [128, CW])
                jk2 = zb("jk2", [128, KC * CW])
                rsc = zb("rsc", [128, CW])
                jkc = zb("jkc", [128, CW])
                gcur = zb("gcur", [128, 256])
                pfold = zb("pfold", [128, 256])
                jnk2 = zb("jnk2", [128, 256])
                ztl = zb("ztl", [128, 4 * 256])
                hqt = zb("hqt", [128, 256])
                scur = zb("scur", [128, 2])
                rsum = zb("rsum", [128, 2])
                svv = zb("svv", [128, 2])
                ps_l1 = ses.enter_context(nc.psum_tensor("zpl1", [128, 512], F32))
                ps_l2 = ses.enter_context(nc.psum_tensor("zpl2", [128, 512], F32))
                ps_dt = [ses.enter_context(nc.psum_tensor(f"zpd{i}", [128, 512], F32))
                         for i in range(4)]
                ps_bc = ses.enter_context(nc.psum_tensor("zpb", [128, 512], F32))
                ps_nr = ses.enter_context(nc.psum_tensor("zpn", [128, 512], F32))

                for kc in range(KC):
                    P.op("gpsimd", lambda e, kc=kc: e.dma_start(
                        out=wdt_sb[:, kc * D:(kc + 1) * D],
                        in_=wdt_h[kc * 128:(kc + 1) * 128, :]), inc=s_ldw, inc_k=16)
                wdt_done = s_ldw.n
                P.op("vector", lambda e: e.memset(gcur[:], 0.0))
                P.op("vector", lambda e: e.memset(pfold[:], 0.0))
                P.op("vector", lambda e: e.memset(scur[:], 1.0), inc=s_v)

                prep_ready = {}
                bc_ready = {}
                step_done = {}
                step_g = {}

                def gen_prep(c):
                    pp = c % 2
                    xw = ((s_v, s_v.n),)
                    for kc in range(KC):
                        P.op("sync", lambda e, kc=kc, c=c, pp=pp: e.dma_start(
                            out=x1c[pp][:, kc * CW:(kc + 1) * CW].rearrange(
                                "p (b t) -> p b t", b=BC),
                            in_=x1s[kc * 128:(kc + 1) * 128, :, c * CT:(c + 1) * CT]),
                            waits=(xw if kc == 0 else ()), inc=s_x1c, inc_k=16)
                    x_done = s_x1c.n
                    P.wait("tensor", s_x1c, x_done)
                    P.wait("tensor", s_v, s_v.n)
                    for kc in range(KC):
                        P.op("tensor", lambda e, kc=kc, pp=pp: e.matmul(
                            ps_l1[:, 0:CW], ones_f[:], x1c[pp][:, kc * CW:(kc + 1) * CW],
                            start=(kc == 0), stop=(kc == KC - 1)),
                            inc=(s_t if kc == KC - 1 else None))
                    sq_inc = [0] * KC
                    P.wait("scalar", s_x1c, x_done)
                    for kc in range(4):
                        P.op("scalar", lambda e, kc=kc, pp=pp: e.activation(
                            sqb2[kc][:], x1c[pp][:, kc * CW:(kc + 1) * CW], AF.Square),
                            inc=s_a)
                        sq_inc[kc] = s_a.n
                    yield
                    P.wait("tensor", s_a, sq_inc[3])
                    P.wait("tensor", s_v, s_v.n)
                    for kc in range(4):
                        P.op("tensor", lambda e, kc=kc: e.matmul(
                            ps_l2[:, 0:CW], ones_f[:], sqb2[kc][:],
                            start=(kc == 0), stop=False))
                    for kc in range(4, 8):
                        P.op("scalar", lambda e, kc=kc, pp=pp: e.activation(
                            sqb2[kc][:], x1c[pp][:, kc * CW:(kc + 1) * CW], AF.Square),
                            inc=s_a)
                        sq_inc[kc] = s_a.n
                    yield
                    P.wait("tensor", s_a, sq_inc[7])
                    for kc in range(4, 8):
                        P.op("tensor", lambda e, kc=kc: e.matmul(
                            ps_l2[:, 0:CW], ones_f[:], sqb2[kc][:],
                            start=False, stop=(kc == KC - 1)),
                            inc=(s_t if kc == KC - 1 else None))
                    l2_done = s_t.n
                    P.wait("vector", s_t, l2_done)
                    P.op("vector", lambda e: e.tensor_scalar(muc[:], ps_l1[:, 0:CW], 1.0 / D, None, ALU.mult))
                    P.op("vector", lambda e: e.tensor_scalar(rsc[:], ps_l2[:, 0:CW], 1.0 / D, None, ALU.mult))
                    P.op("vector", lambda e: e.tensor_tensor(jkc[:], muc[:], muc[:], ALU.mult))
                    P.selfsync("vector", s_v)
                    P.op("vector", lambda e: e.tensor_tensor(rsc[:], rsc[:], jkc[:], ALU.subtract),
                         inc=s_v)
                    var_done = s_v.n
                    yield
                    P.wait("scalar", s_v, var_done)
                    P.op("scalar", lambda e: e.activation(rsc[:], rsc[:], AF.Ln, bias=cst[:, 2:3]),
                         inc=s_a)
                    P.wait("scalar", s_a)
                    P.op("scalar", lambda e: e.activation(rsc[:], rsc[:], AF.Exp, scale=cst[:, 1:2]),
                         inc=s_a)
                    rstd_done = s_a.n
                    P.wait("vector", s_a, rstd_done)
                    for kc in range(KC):
                        sl = slice(kc * CW, (kc + 1) * CW)
                        P.op("vector", lambda e, sl=sl, pp=pp: e.tensor_tensor(
                            u2c[pp][:, sl], x1c[pp][:, sl], muc[:], ALU.subtract))
                    P.selfsync("vector", s_v)
                    yield
                    for kc in range(KC):
                        sl = slice(kc * CW, (kc + 1) * CW)
                        P.op("vector", lambda e, sl=sl, pp=pp: e.tensor_tensor(
                            u2c[pp][:, sl], u2c[pp][:, sl], rsc[:], ALU.mult))
                    P.op("vector", lambda e, pp=pp: e.tensor_copy(u2b[pp][:], u2c[pp][:]),
                         inc=s_v)
                    u2_done = s_v.n
                    yield
                    dt_mm_done = {}
                    ev_inc = {}

                    def dt_group(dc):
                        pd = dc % 4
                        pw = ((s_v, ev_inc[dc - 4]),) if dc >= 4 else ((s_v, u2_done),)
                        for kc in range(KC):
                            P.op("tensor", lambda e, kc=kc, dc=dc, pd=pd, pp=pp: e.matmul(
                                ps_dt[pd][:, 0:CW],
                                wdt_sb[:, kc * D + dc * 128: kc * D + dc * 128 + 128],
                                u2b[pp][:, kc * CW:(kc + 1) * CW],
                                start=(kc == 0), stop=(kc == KC - 1)),
                                waits=(pw if kc == 0 else ()),
                                inc=(s_t if kc == KC - 1 else None))
                        dt_mm_done[dc] = s_t.n

                    def dt_evac(dc):
                        sl = slice(dc * CW, (dc + 1) * CW)
                        P.op("vector", lambda e, sl=sl, pd=dc % 4, dc=dc, pp=pp: e.tensor_scalar(
                            dtc[pp][:, sl], ps_dt[pd][:, 0:CW], bdt_sb[:, dc:dc + 1], None, ALU.add),
                            waits=((s_t, dt_mm_done[dc]),), inc=s_v)
                        ev_inc[dc] = s_v.n

                    dt_group(0)
                    dt_group(1)
                    yield
                    dt_group(2)
                    dt_group(3)
                    dt_evac(0)
                    dt_evac(1)
                    yield
                    dt_group(4)
                    dt_group(5)
                    dt_evac(2)
                    dt_evac(3)
                    yield
                    dt_group(6)
                    dt_group(7)
                    dt_evac(4)
                    dt_evac(5)
                    yield
                    dt_evac(6)
                    dt_evac(7)
                    P.selfsync("vector", s_v)
                    P.op("vector", lambda e, pp=pp: e.tensor_scalar(
                        jk2[:], dtc[pp][:], 0.0, None, ALU.max), inc=s_v)
                    jk_done = s_v.n
                    yield
                    P.wait("scalar", s_v, jk_done)
                    P.op("scalar", lambda e, pp=pp: e.activation(
                        dtc[pp][:], dtc[pp][:], AF.Abs), inc=s_a)
                    P.wait("scalar", s_a)
                    P.op("scalar", lambda e, pp=pp: e.activation(
                        dtc[pp][:], dtc[pp][:], AF.Exp, scale=-1.0), inc=s_a)
                    P.wait("scalar", s_a)
                    P.op("scalar", lambda e, pp=pp: e.activation(
                        dtc[pp][:], dtc[pp][:], AF.Ln, bias=cst[:, 3:4]), inc=s_a)
                    ln_done = s_a.n
                    yield
                    P.wait("vector", s_a, ln_done)
                    P.op("vector", lambda e, pp=pp: e.tensor_tensor(
                        dtc[pp][:], dtc[pp][:], jk2[:], ALU.add), inc=s_v)
                    sp_done = s_v.n
                    yield
                    P.wait("scalar", s_v, sp_done)
                    P.op("scalar", lambda e, pp=pp: e.activation(
                        dtc[pp][:], dtc[pp][:], AF.Exp, scale=SC_SCALE, bias=cst[:, 6:7]), inc=s_a)
                    P.wait("scalar", s_a)
                    P.op("scalar", lambda e, pp=pp: e.activation(
                        dtc[pp][:], dtc[pp][:], AF.Ln, bias=cst[:, 3:4]), inc=s_a)
                    P.wait("scalar", s_a)
                    P.op("scalar", lambda e, pp=pp: e.activation(
                        dtc[pp][:], dtc[pp][:], AF.Exp, scale=-1.0), inc=s_a)
                    ch_done = s_a.n
                    yield
                    P.wait("vector", s_a, ch_done)
                    P.op("vector", lambda e, pp=pp: e.tensor_scalar(
                        dtc[pp][:], dtc[pp][:], -4.0 * SC_R,
                        2.0 * SC_R + SC_C, ALU.mult, ALU.add), inc=s_v)
                    P.wait("vector", s_v)
                    P.op("vector", lambda e, pp=pp: e.tensor_tensor(
                        dxc[pp][:], dtc[pp][:], u2c[pp][:], ALU.mult), inc=s_v)
                    yield
                    prev_btm = [None]

                    def bc_mm(ib):
                        wsb = wb_sb if ib == 0 else wc_sb
                        w = ((s_v, s_v.n),) if ib == 0 else ((s_v, prev_btm[0]),)
                        for kc in range(KC):
                            P.op("tensor", lambda e, kc=kc, wsb=wsb, pp=pp: e.matmul(
                                ps_bc[0:NST, 0:CW], wsb[:, kc * NST:(kc + 1) * NST],
                                u2b[pp][:, kc * CW:(kc + 1) * CW],
                                start=(kc == 0), stop=(kc == KC - 1)),
                                waits=(w if kc == 0 else ()),
                                inc=(s_t if kc == KC - 1 else None))
                        return s_t.n

                    def bc_copy1(done):
                        P.op("vector", lambda e: e.tensor_copy(btc[:], ps_bc[0:NST, 0:CW]),
                             waits=((s_t, done),), inc=s_v)
                        return s_v.n

                    def bc_tr(done):
                        P.op("tensor", lambda e: e.transpose(
                            ps_bc[0:CW, 0:NST], btc[:], identf[0:NST, 0:NST]),
                            waits=((s_v, done),), inc=s_t)
                        return s_t.n

                    def bc_store(ib, done):
                        P.op("vector", lambda e: e.tensor_copy(
                            btm[0:CW, :], ps_bc[0:CW, 0:NST]),
                            waits=((s_t, done), (s_bst, s_bst.n)), inc=s_v)
                        prev_btm[0] = s_v.n
                        P.wait("scalar", s_v)
                        P.op("scalar", lambda e, c=c, ib=ib: e.dma_start(
                            out=bcd[ib, c].rearrange("(q n) -> q n", n=NST),
                            in_=btm[0:CW, :]), inc=s_bst, inc_k=16)
                        return s_bst.n

                    def bc_bcast(ib, done):
                        dstt = bbt[pp] if ib == 0 else cbt[pp]
                        P.wait("gpsimd", s_bst, done)
                        if c >= 2 and (c - 2) in step_done:
                            P.wait("gpsimd", s_v, step_done[c - 2])
                        P.op("gpsimd", lambda e, c=c, ib=ib, dstt=dstt: e.dma_start(
                            out=dstt[:].rearrange("p (o n) -> p o n", o=1),
                            in_=bcd[ib, c].rearrange("(o n) -> o n", o=1)
                            .partition_broadcast(128)), inc=s_bc[c % 2], inc_k=16)

                    d1 = bc_mm(0)
                    yield
                    d2 = bc_copy1(d1)
                    yield
                    d3 = bc_tr(d2)
                    yield
                    d4 = bc_store(0, d3)
                    yield
                    bc_bcast(0, d4)
                    d1 = bc_mm(1)
                    yield
                    d2 = bc_copy1(d1)
                    yield
                    d3 = bc_tr(d2)
                    yield
                    d4 = bc_store(1, d3)
                    yield
                    bc_bcast(1, d4)
                    bc_ready[c] = s_bc[c % 2].n
                    P.wait("vector", s_v, s_v.n)
                    bld = [0, 0, 0, 0]
                    PIECE = CT * 256 // 4
                    da_inc = [[0] * 4 for _ in range(4)]
                    da_params = [
                        dict(func=AF.Exp, scale=-0.125),
                        dict(func=AF.Ln, bias=cst[:, 3:4]),
                        dict(func=AF.Exp, scale=-1.0),
                        dict(func=AF.Exp, scale=cst[:, 5:6], bias=cst[:, 4:5]),
                    ]

                    def build4(base):
                        for st_ in range(base, base + 4):
                            dt_ap = ap_of(dtc[pp], st_, [(CT, BC), (CW, KC), (0, NST)])
                            pa_ap = ap_of(pa_sb, 0, [(0, BC), (NST, KC), (1, NST)])
                            P.op("vector", lambda e, st_=st_, dt_ap=dt_ap, pa_ap=pa_ap, pp=pp:
                                 e.tensor_tensor(
                                     dac[pp][:, st_ * 256:(st_ + 1) * 256].rearrange(
                                         "p (b k n) -> p b k n", b=BC, k=KC),
                                     dt_ap, pa_ap, ALU.mult),
                                 inc=(s_v if st_ % 8 == 7 else None))
                            if st_ % 8 == 7:
                                bld[st_ // 8] = s_v.n

                    def da_piece(p, k):
                        if p == 0:
                            w = ((s_v, bld[k]),)
                        else:
                            w = ((s_a, da_inc[p - 1][k]),)
                        kw = da_params[p]
                        P.op("scalar", lambda e, k=k, kw=kw, pp=pp: e.activation(
                            dac[pp][:, k * PIECE:(k + 1) * PIECE],
                            dac[pp][:, k * PIECE:(k + 1) * PIECE], **kw),
                            waits=w, inc=s_a)
                        da_inc[p][k] = s_a.n

                    for i_ in range(8):
                        build4(4 * i_)
                        yield
                    order = ([(0, k) for k in range(4)] + [(1, k) for k in range(4)]
                             + [(2, k) for k in range(4)] + [(3, k) for k in range(4)])
                    for j_, (p, k) in enumerate(order):
                        da_piece(p, k)
                        if j_ < len(order) - 1:
                            yield
                    prep_ready[c] = s_a.n

                def emit_steps(c, bg=None):
                    pp = c % 2
                    P.wait("vector", s_a, prep_ready[c])
                    P.wait("gpsimd", s_a, prep_ready[c])
                    P.wait("gpsimd", s_bc[c % 2], bc_ready[c])
                    z_inc = {}
                    hq_inc = {-1: s_g.n}
                    gv = {-4: s_v.n, -3: s_v.n, -2: s_v.n, -1: s_v.n}

                    def zsl(t):
                        i = t % 4
                        return ztl[:, i * 256:(i + 1) * 256]

                    def emit_z(t):
                        bb_ap = ap_of(bbt[pp], t * NST, [(CT * NST, BC), (0, KC), (1, NST)])
                        dx_ap = ap_of(dxc[pp], t, [(CT, BC), (CW, KC), (0, NST)])
                        w = ((s_v, gv[t - 4]),) if (t - 4) in gv else ()
                        P.op("gpsimd", lambda e, t=t, dx_ap=dx_ap, bb_ap=bb_ap: e.tensor_tensor(
                            zsl(t).rearrange("p (b k n) -> p b k n", b=BC, k=KC),
                            dx_ap, bb_ap, ALU.mult), waits=w, inc=s_g)
                        z_inc[t] = s_g.n

                    for t in range(min(3, CT)):
                        emit_z(t)
                    s_inc_prev = None
                    for t in range(CT):
                        if t + 3 < CT:
                            emit_z(t + 3)
                        if s_inc_prev is not None:
                            P.wait("vector", s_a, s_inc_prev)
                        gw_ = max(z_inc[t], hq_inc[t - 1])
                        for b in range(BC):
                            P.op("vector", lambda e, b=b, t=t: e.scalar_tensor_tensor(
                                gcur[:, b * 128:(b + 1) * 128],
                                pfold[:, b * 128:(b + 1) * 128], scur[:, b:b + 1],
                                zsl(t)[:, b * 128:(b + 1) * 128], ALU.mult, ALU.add),
                                waits=(((s_g, gw_),) if b == 0 else ()),
                                inc=(s_v if b == BC - 1 else None))
                        if t >= 1:
                            P.op("vector", lambda e, t=t: e.tensor_copy(
                                ap_of(sch[pp], t - 1, [(CT, BC)]), scur[:]))
                        P.wait("vector", s_v)
                        for b in range(BC):
                            P.op("vector", lambda e, b=b: e.scalar_tensor_tensor(
                                jnk2[:, b * 128:(b + 1) * 128],
                                gcur[:, b * 128:(b + 1) * 128], 1.0,
                                gcur[:, b * 128:(b + 1) * 128],
                                ALU.mult, ALU.mult, accum_out=rsum[:, b:b + 1]),
                                inc=(s_v if b == BC - 1 else None))
                        gv[t] = s_v.n
                        gvt = s_v.n
                        P.op("tensor", lambda e: e.matmul(
                            ps_nr[:, 0:2], ones_f[:], rsum[:], start=True, stop=True),
                            waits=((s_v, gvt),), inc=s_t)
                        P.wait("scalar", s_t)
                        P.op("scalar", lambda e: e.activation(svv[:], ps_nr[:, 0:2], AF.Ln),
                             inc=s_a)
                        P.wait("scalar", s_a)
                        P.op("scalar", lambda e: e.activation(
                            svv[:], svv[:], AF.Relu, scale=0.5, bias=cst[:, 7:8]), inc=s_a)
                        P.wait("scalar", s_a)
                        P.op("scalar", lambda e: e.activation(
                            scur[:], svv[:], AF.Exp, scale=-1.0), inc=s_a)
                        s_inc_prev = s_a.n
                        last_global = (c == NCH - 1) and (t == CT - 1)
                        if not last_global:
                            npp = pp if t + 1 < CT else (c + 1) % 2
                            nslot = (t + 1) % CT
                            if t + 1 == CT:
                                P.wait("gpsimd", s_a, prep_ready.get(c + 1, 0))
                            P.op("gpsimd", lambda e, npp=npp, nslot=nslot: e.tensor_tensor(
                                pfold[:],
                                dac[npp][:, nslot * 256:(nslot + 1) * 256],
                                gcur[:], ALU.mult),
                                waits=((s_v, gvt),), inc=s_g)
                        cb_ap = ap_of(cbt[pp], t * NST, [(CT * NST, BC), (0, KC), (1, NST)])
                        P.op("gpsimd", lambda e, cb_ap=cb_ap: e.tensor_tensor(
                            hqt[:].rearrange("p (b k n) -> p b k n", b=BC, k=KC),
                            gcur[:].rearrange("p (b k n) -> p b k n", b=BC, k=KC),
                            cb_ap, ALU.mult),
                            waits=(() if not last_global else ((s_v, gvt),)), inc=s_g)
                        hq_inc[t] = s_g.n
                        yb_ap = ap_of(ybf[pp], t, [(CT, BC), (CW, KC)])
                        P.op("vector", lambda e, yb_ap=yb_ap: e.tensor_reduce(
                            yb_ap, hqt[:].rearrange("p (b k n) -> p b k n", b=BC, k=KC),
                            AX.X, ALU.add), waits=((s_g, hq_inc[t]),), inc=s_v)
                        if bg is not None:
                            nadv = 2 if t < 17 else 1
                            if t == CT - 2:
                                nadv = 10000
                            for _ in range(nadv):
                                if next(bg, "END") == "END":
                                    bg = None
                                    break
                    P.wait("vector", s_a, s_inc_prev)
                    P.op("vector", lambda e: e.tensor_copy(
                        ap_of(sch[pp], CT - 1, [(CT, BC)]), scur[:]), inc=s_v)
                    step_done[c] = s_v.n
                    step_g[c] = s_g.n

                def emit_epi(c):
                    pp = c % 2
                    P.wait("vector", s_x2st, s_x2st.n)
                    P.wait("vector", s_g, step_g[c])
                    P.selfsync("vector", s_v)
                    yb_all = ap_of(ybf[pp], 0, [(CT, BC), (CW, KC), (1, CT)])
                    sc_all = ap_of(sch[pp], 0, [(CT, BC), (0, KC), (1, CT)])
                    P.op("vector", lambda e, yb_all=yb_all, sc_all=sc_all: e.tensor_tensor(
                        yb_all, yb_all, sc_all, ALU.mult))
                    for kc in range(KC):
                        sl = slice(kc * CW, (kc + 1) * CW)
                        P.op("vector", lambda e, sl=sl, kc=kc, pp=pp: e.scalar_tensor_tensor(
                            x2c[pp][:, sl], u2c[pp][:, sl], dp_sb[:, kc:kc + 1],
                            x1c[pp][:, sl], ALU.mult, ALU.add))
                    for kc in range(KC):
                        sl = slice(kc * CW, (kc + 1) * CW)
                        P.op("vector", lambda e, sl=sl, pp=pp: e.tensor_tensor(
                            x2c[pp][:, sl], x2c[pp][:, sl], ybf[pp][:, sl], ALU.add),
                            inc=(s_v if kc == KC - 1 else None))
                    P.wait("scalar", s_v)
                    for kc in range(KC):
                        P.op("scalar", lambda e, kc=kc, c=c, pp=pp: e.dma_start(
                            out=dst_dram[kc * 128:(kc + 1) * 128, :, c * CT:(c + 1) * CT],
                            in_=x2c[pp][:, kc * CW:(kc + 1) * CW].rearrange(
                                "p (b t) -> p b t", b=BC)), inc=s_x2st, inc_k=16)

                sqb2 = [zb(f"sqb2{i}", [128, CW]) for i in range(8)]
                P.wait("tensor", s_ldw, wdt_done)
                for _ in gen_prep(0):
                    pass
                with nc.named_scope("sp00x"):
                    P.flush()
                for c in range(NCH):
                    bg = gen_prep(c + 1) if c + 1 < NCH else None
                    emit_steps(c, bg)
                    if bg is not None:
                        for _ in bg:
                            pass
                    with nc.named_scope(f"ss{c:02d}"):
                        P.flush()
                    emit_epi(c)
                    with nc.named_scope(f"se{c:02d}"):
                        P.flush()
                barrier()
                with nc.named_scope("scanbar"):
                    P.flush()

        if parts in ("f1", "f1dbg"):
            emit_ffn(x_h, out_h, "f1", precast=True)
            if parts == "f1dbg":
                P.op("gpsimd", lambda e: e.dma_start(
                    out=dbg_h[0:4, 0:4096], in_=cwd[:, :]), inc=s_ldw, inc_k=16)
                P.op("gpsimd", lambda e: e.dma_start(
                    out=dbg_h[:, 128:128 + 44 * 512].rearrange("p (h f) -> p h f", f=512),
                    in_=hds[0, 0]), inc=s_ldw, inc_k=16)
                P.op("gpsimd", lambda e: e.dma_start(
                    out=dbg_h[:, 22656:22656 + 5632], in_=w2b[0, 0]), inc=s_ldw, inc_k=16)
                P.wait("gpsimd", s_ldw, s_ldw.n)
        elif parts == "f1scan":
            emit_ffn(x_h, x1s, "f1", precast=True)
            emit_scan(out_h)
        else:
            emit_ffn(x_h, x1s, "f1", precast=True)
            emit_scan(x2s)
            emit_ffn(x2s, out_h, "f2", precast=False)
        for ss_ in store_sems:
            P.wait("scalar", ss_, ss_.n)
        P.flush()

    return nc


def _stage_weights(w1, w3, w2, Wdt, WB, WC, gate_w, A_log, bdt, Dp):
    w1r = np.ascontiguousarray(
        np.asarray(w1[0:EXP], dtype=np.float32)
        .reshape(EXP, KC, 128, HC, 128).transpose(0, 3, 2, 1, 4))
    w3r = np.ascontiguousarray(
        np.asarray(w3[0:EXP], dtype=np.float32)
        .reshape(EXP, KC, 128, HC, 128).transpose(0, 3, 2, 1, 4))
    w13 = np.ascontiguousarray(np.stack([w1r, w3r], axis=0))
    w2r = np.ascontiguousarray(
        np.asarray(w2[0:EXP], dtype=np.float32)
        .reshape(EXP, HC, 128, KC, 128).transpose(0, 3, 2, 1, 4)
        .reshape(EXP, KC, 128, HC * 128))
    return dict(
        w13=w13, w2=w2r,
        wdt=np.ascontiguousarray(np.asarray(Wdt, dtype=np.float32).T),
        wb=np.ascontiguousarray(np.asarray(WB, dtype=np.float32).T),
        wc=np.ascontiguousarray(np.asarray(WC, dtype=np.float32).T),
        gw=np.ascontiguousarray(np.asarray(gate_w[0:EXP], dtype=np.float32).T),
        alog=np.ascontiguousarray(np.asarray(A_log, dtype=np.float32)),
        bdt=np.ascontiguousarray(np.asarray(bdt, dtype=np.float32).reshape(KC, 128).T),
        dp=np.ascontiguousarray(np.asarray(Dp, dtype=np.float32).reshape(KC, 128).T),
    )


def kernel(L0_attn_x, L0_attn_Wdt, L0_attn_bdt, L0_attn_WB, L0_attn_WC,
           L0_attn_A_log, L0_attn_Dp, L0_ffn_gate_w, L0_ffn_w1, L0_ffn_w2,
           L0_ffn_w3):
    ncores = 8
    x = np.asarray(L0_attn_x, dtype=np.float32)
    shared = _stage_weights(L0_ffn_w1, L0_ffn_w3, L0_ffn_w2, L0_attn_Wdt,
                            L0_attn_WB, L0_attn_WC, L0_ffn_gate_w,
                            L0_attn_A_log, L0_attn_bdt, L0_attn_Dp)
    in_maps = []
    for c in range(ncores):
        m = {"x": np.ascontiguousarray(x[2 * c:2 * c + 2].transpose(2, 0, 1))}
        m.update(shared)
        in_maps.append(m)
    nc = build_program()
    res = run_bass_kernel_spmd(nc, in_maps, core_ids=list(range(ncores)))
    outs = [np.asarray(res.results[c]["out"]).transpose(1, 2, 0)
            for c in range(ncores)]
    return np.ascontiguousarray(np.concatenate(outs, axis=0))
